# version 29
# baseline (speedup 1.0000x reference)
import contextlib
import numpy as np
import concourse.bass as bass
import concourse.mybir as mybir
from concourse.alu_op_type import AluOpType
from concourse.bass_utils import run_bass_kernel_spmd

F32 = mybir.dt.float32
BF16 = mybir.dt.bfloat16
AF = mybir.ActivationFunctionType
OP = AluOpType

B_FULL = 8
SEQ = 4096
D = 1024
DI = 2048
E2 = 4096
DFF = 4096
NST = 16
DTR = 64
LC = 256
NCHUNK = SEQ // LC
NTT = LC // 128
NH = 8
LN_EPS = 1e-5

DMA_SEMS = ('d_su_s', 'd_su_g', 'd_cv_win', 'd_cv_wout', 'd_cv_w1', 'd_cv_w2', 'd_cv_nx', 'd_xin', 'd_xre', 'd_xout',
            'd_win0', 'd_win1', 'd_wout', 'd_w10', 'd_w11', 'd_w20', 'd_w21',
            'd_at0', 'd_at1', 'd_nx0', 'd_nx1',
            'd_dto0', 'd_dto1', 'd_xco0', 'd_xco1',
            'd_dti0', 'd_dti1', 'd_dti2', 'd_xci0', 'd_xci1', 'd_xci2')


class Emitter:

    def __init__(self):
        self.ops = []
        self.cnt = {}
        self.writer = {}
        self.readers = {}
        self.pending_reads = {}
        self.pending_writes = {}
        self.watermark = {}
        for e in ('sync', 'scalar', 'vector', 'tensor', 'gpsimd'):
            self.cnt['e_' + e] = 0
            self.pending_reads[e] = []
            self.pending_writes[e] = []
        for dn in DMA_SEMS:
            self.cnt[dn] = 0

    def _deps(self, reads, writes):
        deps = {}
        for b in reads:
            t = self.writer.get(b)
            if t:
                deps[t[0]] = max(deps.get(t[0], 0), t[1])
        for b in writes:
            t = self.writer.get(b)
            if t:
                deps[t[0]] = max(deps.get(t[0], 0), t[1])
            for t in self.readers.get(b, ()):
                deps[t[0]] = max(deps.get(t[0], 0), t[1])
        return deps

    def emit(self, eng, build, reads=(), writes=(), inc=True, dma_sem=None, war=(), est=None, tab=None, slot=None):
        reads = list(reads)
        writes = list(writes)
        deps = self._deps(reads, list(writes) + list(war))
        waits = []
        for s, v in deps.items():
            if s.startswith('d_'):
                v = self.cnt[s]
            if self.watermark.get((eng, s), -1) < v:
                waits.append((s, v))
                self.watermark[(eng, s)] = v
        if dma_sem is not None:
            self.cnt[dma_sem] += 16
            token = (dma_sem, self.cnt[dma_sem])
        elif inc:
            self.cnt['e_' + eng] += 1
            token = ('e_' + eng, self.cnt['e_' + eng])
        else:
            token = None

        if token is not None:
            reads += self.pending_reads[eng]
            writes_all = writes + self.pending_writes[eng]
            self.pending_reads[eng] = []
            self.pending_writes[eng] = []
            for b in reads:
                self.readers.setdefault(b, set()).add(token)
            for b in writes_all:
                self.writer[b] = token
                self.readers[b] = set()
        else:
            self.pending_reads[eng] += reads
            self.pending_writes[eng] += writes

        self.ops.append((eng, build, waits, token, dma_sem))
        return token

    def fence_all(self):
        waits = [(k, v) for k, v in self.cnt.items() if v > 0]
        for e in ('sync', 'scalar', 'vector', 'tensor', 'gpsimd'):
            mywaits = []
            for s, v in waits:
                if self.watermark.get((e, s), -1) < v:
                    mywaits.append((s, v))
                    self.watermark[(e, s)] = v
            self.ops.append((e, lambda s: None, mywaits, None, None))

    def run(self, eng_name, section, sems):
        for e, build, waits, token, dma_sem in self.ops:
            if e != eng_name:
                continue
            for s, v in waits:
                section.wait_ge(sems[s], v)
            ins = build(section)
            if token is not None and ins is not None:
                if dma_sem is not None:
                    ins.then_inc(sems[dma_sem], 16)
                else:
                    ins.then_inc(sems[token[0]], 1)


def TH(lst, eng, build, **kw):
    lst.append((eng, build, kw))


ENGS = ('sync', 'scalar', 'vector', 'tensor', 'gpsimd')
DEF_EST = {'sync': 150, 'scalar': 520, 'vector': 520, 'tensor': 190, 'gpsimd': 1100}


def schedule_ops(prog):
    n = len(prog)
    deps = [set() for _ in range(n)]
    writer = {}
    readers = {}
    pend_r = {e: [] for e in ENGS}
    pend_w = {e: [] for e in ENGS}
    pend_idx = {e: [] for e in ENGS}
    for idx, (eng, build, kw) in enumerate(prog):
        reads = list(kw.get('reads', ()))
        writes = list(kw.get('writes', ()))
        war = list(kw.get('war', ()))
        dma = kw.get('dma_sem')
        inc = kw.get('inc', True)
        for b in reads:
            w = writer.get(b)
            if w is not None:
                deps[idx].add(w)
        for b in writes + war:
            w = writer.get(b)
            if w is not None:
                deps[idx].add(w)
            for rd in readers.get(b, ()):
                deps[idx].add(rd)
        if dma is not None or inc:
            allr = reads + pend_r[eng]
            allw = writes + pend_w[eng]
            for j in pend_idx[eng]:
                deps[idx].add(j)
            pend_r[eng] = []
            pend_w[eng] = []
            pend_idx[eng] = []
            for b in allr:
                readers.setdefault(b, set()).add(idx)
            for b in allw:
                writer[b] = idx
                readers[b] = set()
        else:
            pend_r[eng] += reads
            pend_w[eng] += writes
            pend_idx[eng].append(idx)
        deps[idx].discard(idx)

    unit_of = [None] * n
    units = []
    openu = {e: None for e in ENGS}
    for idx, (eng, build, kw) in enumerate(prog):
        inc = kw.get('inc', True)
        dma = kw.get('dma_sem')
        if openu[eng] is not None:
            u = openu[eng]
        else:
            u = len(units)
            units.append([])
            openu[eng] = u
        units[u].append(idx)
        unit_of[idx] = u
        if inc or dma is not None:
            openu[eng] = None
    nu = len(units)
    udeps = [set() for _ in range(nu)]
    for idx in range(n):
        u = unit_of[idx]
        for d in deps[idx]:
            du = unit_of[d]
            if du != u:
                udeps[u].add(du)
    ueng = [prog[units[u][0]][0] for u in range(nu)]
    utab = []
    for u in range(nu):
        t = None
        if ueng[u] == 'scalar':
            t = 'e'
            for idx in units[u]:
                t2 = prog[idx][2].get('tab')
                if t2:
                    t = t2
        utab.append(t)
    udur = []
    uisdma = []
    for u in range(nu):
        d = 0
        isdma = False
        for idx in units[u]:
            kw = prog[idx][2]
            if kw.get('dma_sem') is not None:
                isdma = True
                d += kw.get('est', 5000)
            else:
                d += kw.get('est', DEF_EST[prog[idx][0]])
        udur.append(d)
        uisdma.append(isdma)
    succ = [[] for _ in range(nu)]
    for u in range(nu):
        for d in udeps[u]:
            succ[d].append(u)
    height = [0] * nu
    for u in range(nu - 1, -1, -1):
        h = 0
        for s2 in succ[u]:
            if height[s2] > h:
                h = height[s2]
        height[u] = h + udur[u]
    maxslot = 0
    uslot = []
    for u in range(nu):
        t = None
        for idx in units[u]:
            t2 = prog[idx][2].get('slot')
            if t2 is not None:
                t = t2 if t is None else min(t, t2)
        uslot.append(t)
        if t is not None and t > maxslot:
            maxslot = t
    WAGE = 25000
    for u in range(nu):
        if uslot[u] is not None and ueng[u] != 'vector':
            height[u] += WAGE * (maxslot - uslot[u])
    ndeps = [len(udeps[u]) for u in range(nu)]
    done_t = [0.0] * nu
    ready_t = [0.0] * nu
    avail = {e: 0.0 for e in ENGS}
    remaining = {e: 0 for e in ENGS}
    for u in range(nu):
        remaining[ueng[u]] += 1
    ready = {e: [] for e in ENGS}
    for u in range(nu):
        if ndeps[u] == 0:
            ready[ueng[u]].append(u)
    order = []
    scheduled = [False] * nu
    cur_tab = ['e']
    INF = float('inf')
    while len(order) < nu:
        e = None
        for e2 in ENGS:
            if remaining[e2] and ready[e2]:
                if e is None or avail[e2] < avail[e]:
                    e = e2
        if e is None:
            best = None
            for e2 in ENGS:
                for u in ready[e2]:
                    if best is None or ready_t[u] < ready_t[best]:
                        best = u
            assert best is None
            break
        now = avail[e]
        cands = [u for u in ready[e] if ready_t[u] <= now]
        if cands:
            if e == 'scalar':
                u = max(cands, key=lambda x: height[x]
                        + (2600 if utab[x] == cur_tab[0] else 0))
            else:
                u = max(cands, key=lambda x: height[x])
        else:
            u = min(ready[e], key=lambda x: (ready_t[x], -height[x]))
        ready[e].remove(u)
        scheduled[u] = True
        remaining[e] -= 1
        start = max(avail[e], ready_t[u])
        dur = udur[u]
        if e == 'scalar' and utab[u] is not None:
            if utab[u] != cur_tab[0]:
                dur += 1300
            cur_tab[0] = utab[u]
        if uisdma[u]:
            avail[e] = start + 200
        else:
            avail[e] = start + dur
        done_t[u] = start + dur
        order.append(u)
        for s2 in succ[u]:
            ndeps[s2] -= 1
            if done_t[u] > ready_t[s2]:
                ready_t[s2] = done_t[u]
            if ndeps[s2] == 0:
                ready[ueng[s2]].append(s2)
    assert len(order) == nu, (len(order), nu)
    out = []
    for u in order:
        for idx in units[u]:
            out.append(prog[idx])
    return out



def build_nc():
    nc = bass.Bass('TRN2', target_bir_lowering=False)

    dx = nc.dram_tensor("x", [SEQ, D], F32, kind="ExternalInput").ap()
    d_wint = nc.dram_tensor("W_inT", [D, E2], F32, kind="ExternalInput").ap()
    d_convw = nc.dram_tensor("convw", [128, 64], F32, kind="ExternalInput").ap()
    d_convb = nc.dram_tensor("convb", [128, 16], F32, kind="ExternalInput").ap()
    d_wxt = nc.dram_tensor("W_xTp", [DI, 96], F32, kind="ExternalInput").ap()
    d_wdtt = nc.dram_tensor("W_dtT", [DTR, DI], F32, kind="ExternalInput").ap()
    d_bdt = nc.dram_tensor("b_dt", [128, 16], F32, kind="ExternalInput").ap()
    d_alog = nc.dram_tensor("A_log", [128, 256], F32, kind="ExternalInput").ap()
    d_dp = nc.dram_tensor("Dp", [128, 16], F32, kind="ExternalInput").ap()
    d_woutt = nc.dram_tensor("W_outT", [DI, D], F32, kind="ExternalInput").ap()
    d_lng = nc.dram_tensor("ln_g", [128, 8], F32, kind="ExternalInput").ap()
    d_lnb = nc.dram_tensor("ln_b", [128, 8], F32, kind="ExternalInput").ap()
    d_attnw = nc.dram_tensor("attn_w", [D, D], F32, kind="ExternalInput").ap()
    d_attnb = nc.dram_tensor("attn_b", [128, 8], F32, kind="ExternalInput").ap()
    d_w1 = [nc.dram_tensor("w1A", [D, DFF], F32, kind="ExternalInput").ap(),
            nc.dram_tensor("w1B", [D, DFF], F32, kind="ExternalInput").ap()]
    d_b1 = [nc.dram_tensor("b1A", [128, 32], F32, kind="ExternalInput").ap(),
            nc.dram_tensor("b1B", [128, 32], F32, kind="ExternalInput").ap()]
    d_w2 = [nc.dram_tensor("w2A", [DFF, D], F32, kind="ExternalInput").ap(),
            nc.dram_tensor("w2B", [DFF, D], F32, kind="ExternalInput").ap()]
    d_b2 = [nc.dram_tensor("b2A", [128, 8], F32, kind="ExternalInput").ap(),
            nc.dram_tensor("b2B", [128, 8], F32, kind="ExternalInput").ap()]
    d_nextw = nc.dram_tensor("next_w", [D, D], F32, kind="ExternalInput").ap()
    d_nextb = nc.dram_tensor("next_b", [128, 8], F32, kind="ExternalInput").ap()
    d_out = nc.dram_tensor("out", [SEQ, D], F32, kind="ExternalOutput").ap()

    d_dts = nc.dram_tensor("d_dts", [NCHUNK * 16 * 128, LC], F32).ap()
    d_xcs = nc.dram_tensor("d_xcs", [NCHUNK * 16 * 128, LC], F32).ap()
    d_winb = nc.dram_tensor("d_winb", [D, E2], BF16).ap()
    d_woutb = nc.dram_tensor("d_woutb", [DI, D], BF16).ap()
    d_w1b = [nc.dram_tensor("d_w1bA", [D, DFF], BF16).ap(),
             nc.dram_tensor("d_w1bB", [D, DFF], BF16).ap()]
    d_w2b = [nc.dram_tensor("d_w2bA", [DFF, D], BF16).ap(),
             nc.dram_tensor("d_w2bB", [DFF, D], BF16).ap()]
    d_attnbf = nc.dram_tensor("d_attnbf", [D, D], BF16).ap()
    d_nxwbf = nc.dram_tensor("d_nxwbf", [D, D], BF16).ap()

    em = Emitter()
    E = em.emit

    with contextlib.ExitStack() as _st:
        _A = _st.enter_context
        wxt_bf = _A(nc.sbuf_tensor("wxt_bf", [128, 16 * 96], BF16))
        wdtt_bf = _A(nc.sbuf_tensor("wdtt_bf", [DTR, DI], BF16))
        a_pos = _A(nc.sbuf_tensor("a_pos", [128, 256], F32))
        sel = _A(nc.sbuf_tensor("sel", [32, 32 * 128], BF16))
        identf = _A(nc.sbuf_tensor("identf", [128, 128], F32))
        identb = _A(nc.sbuf_tensor("identb", [128, 128], BF16))
        diagdp = _A(nc.sbuf_tensor("diagdp", [128, 16 * 128], BF16))
        convw_sb = _A(nc.sbuf_tensor("convw_sb", [128, 64], F32))
        dp_sb = _A(nc.sbuf_tensor("dp_sb", [128, 16], F32))
        negbdt = _A(nc.sbuf_tensor("negbdt", [128, 16], F32))
        convb_sb = _A(nc.sbuf_tensor("convb_sb", [128, 16], F32))
        biasp = _A(nc.sbuf_tensor("biasp", [128, 8], F32))
        nextb_sb = _A(nc.sbuf_tensor("nextb_sb", [128, 8], F32))
        b1_sb = _A(nc.sbuf_tensor("b1_sb", [128, 64], F32))
        b2_sb = _A(nc.sbuf_tensor("b2_sb", [128, 16], F32))
        lng_sb = _A(nc.sbuf_tensor("lng_sb", [128, 8], F32))
        ones_bf = _A(nc.sbuf_tensor("ones_bf", [128, 128], BF16))
        eps_sb = _A(nc.sbuf_tensor("eps_sb", [128, 1], F32))
        one_sb = _A(nc.sbuf_tensor("one_sb", [128, 1], F32))
        q_eng = {('e_' + e): _A(nc.semaphore('e_' + e))
                 for e in ('sync', 'scalar', 'vector', 'tensor', 'gpsimd')}
        dma_q = {dn: _A(nc.semaphore(dn)) for dn in DMA_SEMS}
        sems = dict(q_eng)
        sems.update(dma_q)

        with contextlib.ExitStack() as _st0:
            stage = _st0.enter_context(nc.sbuf_tensor("stage", [128, 8 * D], F32))
            attn_st = _st0.enter_context(nc.sbuf_tensor("attn_st", [128, 8 * D], BF16))
            lnb_bf = _st0.enter_context(nc.sbuf_tensor("lnb_bf", [128, 8], BF16))
            attnb_st = _st0.enter_context(nc.sbuf_tensor("attnb_st", [128, 8], F32))
            ps0 = _st0.enter_context(nc.psum_tensor("ps0", [128, 512], F32))

            for dst, key, src in [(a_pos, 'a_pos', d_alog), (negbdt, 'negbdt', d_bdt),
                                  (convb_sb, 'convb', d_convb), (nextb_sb, 'nextb', d_nextb),
                                  (lng_sb, 'lng', d_lng), (attnb_st, 'attnb', d_attnb),
                                  (convw_sb, 'convw', d_convw), (dp_sb, 'dp', d_dp)]:
                E('sync', lambda s, dst=dst, src=src: s.dma_start(dst[:, :], src),
                  writes=[key], dma_sem='d_su_s')
            E('sync', lambda s: s.dma_start(b1_sb[:, 0:32], d_b1[0]), writes=['b1'], dma_sem='d_su_s')
            E('sync', lambda s: s.dma_start(b1_sb[:, 32:64], d_b1[1]), writes=['b1'], dma_sem='d_su_s')
            E('sync', lambda s: s.dma_start(b2_sb[:, 0:8], d_b2[0]), writes=['b2'], dma_sem='d_su_s')
            E('sync', lambda s: s.dma_start(b2_sb[:, 8:16], d_b2[1]), writes=['b2'], dma_sem='d_su_s')
            E('sync', lambda s: s.dma_start(stage[:, 80:88], d_lnb), writes=['stage_t'], dma_sem='d_su_s')

            E('gpsimd', lambda g: g.dma_start(d_winb, d_wint), writes=['cv_win'], dma_sem='d_cv_win')
            for g_i in range(16):
                E('gpsimd', lambda g, g_i=g_i: g.dma_start(
                    wxt_bf[:, g_i * 96:(g_i + 1) * 96], d_wxt[g_i * 128:(g_i + 1) * 128, :]),
                  writes=['wxt'], dma_sem='d_su_g')
            E('gpsimd', lambda g: g.dma_start(wdtt_bf[:, :], d_wdtt), writes=['wdtt'], dma_sem='d_su_g')

            E('scalar', lambda sc: sc.activation(a_pos[:, :], a_pos[:, :], AF.Exp),
              reads=['a_pos'], writes=['a_pos'])
            E('vector', lambda v: v.tensor_scalar_mul(a_pos[:, :], a_pos[:, :], -1.0),
              reads=['a_pos'], writes=['a_pos'])
            E('vector', lambda v: v.tensor_copy(lnb_bf[:, :], stage[:, 80:88]),
              reads=['stage_t'], writes=['lnb_bf'])

            for t, k in ((identf, 'identf'), (identb, 'identb')):
                E('gpsimd', lambda g, t=t: g.memset(t[:, :], 1.0), writes=[k])
                E('gpsimd', lambda g, t=t: g.affine_select(
                    t[:, :], t[:, :], [[1, 128]], OP.is_equal, 0.0, base=0,
                    channel_multiplier=-1), reads=[k], writes=[k])
            E('gpsimd', lambda g: g.memset(ones_bf[:, :], 1.0), writes=['ones_bf'])
            E('gpsimd', lambda g: g.memset(eps_sb[:, :], LN_EPS), writes=['eps'])
            E('gpsimd', lambda g: g.memset(one_sb[:, :], 1.0), writes=['one'])
            E('vector', lambda v: v.memset(sel[:, :], 1.0), writes=['sel'])
            E('gpsimd', lambda g: g.affine_select(
                sel[:, :].rearrange("p (t j) -> p t j", j=128),
                sel[:, :].rearrange("p (t j) -> p t j", j=128),
                [[1, 32], [0, 128]], OP.is_equal, 0.0, base=0, channel_multiplier=-1),
              reads=['sel'], writes=['sel'])
            for g_i in range(16):
                E('scalar', lambda sc, g_i=g_i: sc.activation(
                    diagdp[:, g_i * 128:(g_i + 1) * 128], ones_bf[:, :], AF.Copy,
                    scale=dp_sb[:, g_i:g_i + 1]),
                  reads=['ones_bf', 'dp'], writes=['diagdp'])
            E('gpsimd', lambda g: g.affine_select(
                diagdp[:, :].rearrange("p (t f) -> p t f", f=128),
                diagdp[:, :].rearrange("p (t f) -> p t f", f=128),
                [[0, 16], [1, 128]], OP.is_equal, 0.0, base=0, channel_multiplier=-1),
              reads=['diagdp'], writes=['diagdp'])

            for dc in range(8):
                E('sync', lambda s, dc=dc: s.dma_start(
                    stage[:, dc * D:(dc + 1) * D], d_attnw[dc * 128:(dc + 1) * 128, :]),
                  writes=['stage', 'stage_t'], dma_sem='d_su_s')
            E('vector', lambda v: v.tensor_copy(attn_st[:, :], stage[:, :]),
              reads=['stage'], writes=['attn_st'])
            for ec in range(8):
                for dc in range(8):
                    E('tensor', lambda te, ec=ec, dc=dc: te.matmul(
                        ps0[:, ec:ec + 1],
                        attn_st[:, dc * D + ec * 128: dc * D + (ec + 1) * 128],
                        lnb_bf[:, dc:dc + 1],
                        start=(dc == 0), stop=(dc == 7)),
                      reads=['lnb_bf', 'attn_st'], writes=['ps0'], inc=(dc == 7))
            E('scalar', lambda sc: sc.copy(biasp[:, :], ps0[:, 0:8]),
              reads=['ps0'], writes=['biasp'])
            E('vector', lambda v: v.tensor_add(biasp[:, :], biasp[:, :], attnb_st[:, :]),
              reads=['biasp', 'attnb'], writes=['biasp'])
            for dc in range(8):
                E('vector', lambda v, dc=dc: v.tensor_scalar(
                    attn_st[:, dc * D:(dc + 1) * D], stage[:, dc * D:(dc + 1) * D],
                    lng_sb[:, dc:dc + 1], None, OP.mult),
                  reads=['stage', 'lng'], writes=['attn_st'])
            E('sync', lambda s: s.dma_start(
                d_attnbf.rearrange("(dc p) e -> p dc e", p=128),
                attn_st[:, :].rearrange("p (dc e) -> p dc e", dc=8)),
              reads=['attn_st'], writes=['cv_at'], dma_sem='d_su_s')

            with nc.Block() as blk0:
                @blk0.sync
                def _(s):
                    em.run('sync', s, sems)

                @blk0.scalar
                def _(s):
                    em.run('scalar', s, sems)

                @blk0.vector
                def _(s):
                    em.run('vector', s, sems)

                @blk0.tensor
                def _(s):
                    em.run('tensor', s, sems)

                @blk0.gpsimd
                def _(s):
                    em.run('gpsimd', s, sems)
            em.ops = []
        em.fence_all()

        with contextlib.ExitStack() as _st1:
            _M = _st1.enter_context
            xn_fm = _M(nc.sbuf_tensor("xn_fm", [128, 2 * 8 * LC], BF16))
            ln2_fmb = _M(nc.sbuf_tensor("ln2_fmb", [128, 2 * 8 * LC], BF16))
            sz_bf = _M(nc.sbuf_tensor("sz_bf", [128, 2 * 16 * LC], BF16))
            gated = _M(nc.sbuf_tensor("gated", [128, 2 * 16 * LC], BF16))
            bc_bf = _M(nc.sbuf_tensor("bc_bf", [32, 2 * LC], BF16))
            dtlow_bf = _M(nc.sbuf_tensor("dtlow_bf", [DTR, LC], BF16))
            s_fm = _M(nc.sbuf_tensor("s_fm", [128, 8 * LC], F32))
            hcarry = _M(nc.sbuf_tensor("hcarry", [128, 256], F32))
            halo_sb = _M(nc.sbuf_tensor("halo_sb", [128, 16 * 3], F32))
            stats = _M(nc.sbuf_tensor("stats", [128, 32], F32))
            ball = _M(nc.sbuf_tensor("ball", [128, 2 * 16 * LC], BF16))
            call = _M(nc.sbuf_tensor("call", [128, 2 * 16 * LC], BF16))
            da0 = _M(nc.sbuf_tensor("da0", [128, NH * (LC + 1)], F32))
            da1 = _M(nc.sbuf_tensor("da1", [128, NH * (LC + 1)], F32))
            db0 = _M(nc.sbuf_tensor("db0", [128, NH * (LC + 1)], F32))
            db1 = _M(nc.sbuf_tensor("db1", [128, NH * (LC + 1)], F32))
            da_p = [da0, da1]
            dbx_p = [db0, db1]
            hc_p = [db0[:, :].bitcast(BF16)[:, 0:NH * LC],
                    db1[:, :].bitcast(BF16)[:, 0:NH * LC]]
            dtin = _M(nc.sbuf_tensor("dtin", [128, 3 * LC], F32))
            xcin = _M(nc.sbuf_tensor("xcin", [128, 3 * LC], F32))
            u_r = _M(nc.sbuf_tensor("u_r", [128, 2 * LC], F32))
            xcbf_r = _M(nc.sbuf_tensor("xcbf_r", [128, 2 * LC], BF16))
            xdbl_sb = _M(nc.sbuf_tensor("xdbl_sb", [96, LC], F32))
            pool = _M(nc.sbuf_tensor("pool", [128, 26624], BF16))

            def pv_f32(off_b, nelem):
                k = off_b // 2
                return pool[:, k:k + 2 * nelem].bitcast(F32)

            def pv_bf(off_b, nelem):
                k = off_b // 2
                return pool[:, k:k + nelem]

            x_tm = pv_f32(0, NTT * D)
            xn_tm = pv_f32(8192, NTT * D)
            ln2_tm = pv_f32(16384, NTT * D)
            trash = pv_f32(24576, D)
            winq = [pv_bf(0, 8 * 512), pv_bf(8192, 8 * 512)]
            xcfull = pv_f32(16384, 16 * (LC + 3))
            xcv_r = [pv_f32(33536, LC), pv_f32(34560, LC)]
            dtr_r = [pv_f32(35584, LC), pv_f32(36608, LC)]
            xcb_r = [pv_bf(37632, LC), pv_bf(38144, LC)]
            woutb = pv_bf(0, 16 * 1024)
            w1r = [pv_bf(0, 8 * 512), pv_bf(8192, 8 * 512)]
            w2r = [pv_bf(16384, 4 * 1024), pv_bf(24576, 4 * 1024)]
            h1r = [pv_bf(32768, 4 * LC), pv_bf(34816, 4 * LC)]
            xa_bf = pv_bf(36864, 8 * LC)
            y2_bf = pv_bf(40960, 8 * LC)
            gg = pv_bf(45056, 8 * LC)
            attn_r = [pv_bf(49152, 8 * 128), pv_bf(51200, 8 * 128)]
            next_r = attn_r
            xre = pv_f32(0, NTT * D)
            out_tm = pv_f32(8192, NTT * D)
            comb = pv_f32(16384, 8 * LC)

            POOL_REGION = {
                'x_tm': (0, 8192), 'xn_tm': (8192, 16384), 'ln2_tm': (16384, 24576),
                'trash': (24576, 28672),
                'winq0': (0, 8192), 'winq1': (8192, 16384), 'xcfull': (16384, 33152),
                'xcv0': (33536, 34560), 'xcv1': (34560, 35584),
                'dtr0': (35584, 36608), 'dtr1': (36608, 37632),
                'xcb0': (37632, 38144), 'xcb1': (38144, 38656),
                'woutb': (0, 32768),
                'w1b0': (0, 8192), 'w1b1': (8192, 16384),
                'w2b0': (16384, 24576), 'w2b1': (24576, 32768),
                'h10': (32768, 34816), 'h11': (34816, 36864),
                'xa': (36864, 40960), 'y2': (40960, 45056),
                'gg': (45056, 49152),
                'at0': (49152, 51200), 'at1': (51200, 53248),
                'xre': (0, 8192), 'out_tm': (8192, 16384), 'comb': (16384, 24576),
            }

            def povl(key):
                a0, a1 = POOL_REGION[key]
                return [k for k, (b0, b1) in POOL_REGION.items()
                        if k != key and b0 < a1 and a0 < b1]

            psA = _M(nc.psum_tensor("psA", [128, 512], F32))
            psB = _M(nc.psum_tensor("psB", [128, 512], F32))
            psY0 = _M(nc.psum_tensor("psY0", [128, 512], F32))
            psY1 = _M(nc.psum_tensor("psY1", [128, 512], F32))
            psY_p = [(psY0, 'psY0'), (psY1, 'psY1')]
            psY2 = _M(nc.psum_tensor("psY2", [128, 8 * LC], F32))
            GPS = [(psA, 'psA'), (psB, 'psB')]
            gps_i = [0]

            def P():
                ps, pn = GPS[gps_i[0] % 2]
                gps_i[0] += 1
                return ps, pn

            def phase_A(c):
                L = []
                par = c % 2
                t0 = c * LC
                xnp = xn_fm[:, par * 8 * LC:(par + 1) * 8 * LC]
                ln2p = ln2_fmb[:, par * 8 * LC:(par + 1) * 8 * LC]
                xnk, ln2k = f'xn{par}', f'ln2{par}'
                TH(L, 'sync', lambda s, t0=t0: s.dma_start(
                    x_tm[:, :].rearrange("p (tt d) -> p tt d", tt=NTT),
                    dx[t0:t0 + LC, :].rearrange("(tt p) d -> p tt d", p=128)),
                   writes=['x_tm'], dma_sem='d_xin', war=povl('x_tm'), est=7000)
                for src, dstk, dst in ((x_tm, 'xn_tm', xn_tm), (xn_tm, 'ln2_tm', ln2_tm)):
                    srck = 'x_tm' if src is x_tm else 'xn_tm'
                    for tt in range(NTT):
                        for hh in range(2):
                            TH(L, 'vector', lambda v, tt=tt, hh=hh, src=src: v.bn_stats(
                                stats[:, 12 * tt + 6 * hh:12 * tt + 6 * hh + 6],
                                src[:, tt * D + hh * 512: tt * D + (hh + 1) * 512]),
                               reads=[srck], writes=['stats'], est=680)
                        TH(L, 'vector', lambda v, tt=tt: v.bn_aggr(
                            stats[:, 24 + 2 * tt:26 + 2 * tt], stats[:, 12 * tt:12 * tt + 12]),
                           reads=['stats'], writes=['stats'], est=300)
                    TH(L, 'scalar', lambda sc: sc.activation(
                        stats[:, 25:28:2], stats[:, 25:28:2], AF.Ln,
                        bias=eps_sb[:, 0:1]),
                       reads=['stats', 'eps'], writes=['stats'])
                    TH(L, 'scalar', lambda sc: sc.activation(
                        stats[:, 25:28:2], stats[:, 25:28:2], AF.Exp,
                        scale=-0.5),
                       reads=['stats'], writes=['stats'])
                    for tt in range(NTT):
                        TH(L, 'vector', lambda v, tt=tt, src=src, dst=dst: v.tensor_scalar(
                            dst[:, tt * D:(tt + 1) * D], src[:, tt * D:(tt + 1) * D],
                            stats[:, 24 + 2 * tt:25 + 2 * tt], stats[:, 25 + 2 * tt:26 + 2 * tt],
                            OP.subtract, OP.mult),
                           reads=[srck, 'stats'], writes=[dstk],
                           war=(povl(dstk) if tt == 0 else ()))
                for src, srck, dst, dstk in ((xn_tm, 'xn_tm', xnp, xnk),
                                             (ln2_tm, 'ln2_tm', ln2p, ln2k)):
                    for tt in range(NTT):
                        for dc in range(8):
                            ps, pn = P()
                            TH(L, 'tensor', lambda te, tt=tt, dc=dc, ps=ps, src=src: te.transpose(
                                ps[:, 0:128], src[:, tt * D + dc * 128: tt * D + (dc + 1) * 128],
                                identf[:, :]),
                               reads=[srck, 'identf'], writes=[pn])
                            TH(L, 'scalar', lambda sc, tt=tt, dc=dc, ps=ps, dst=dst: sc.copy(
                                dst[:, dc * LC + tt * 128: dc * LC + tt * 128 + 128], ps[:, 0:128]),
                               reads=[pn], writes=[dstk])
                return L

            def phase_B(c):
                L = []
                par = c % 2
                szp = sz_bf[:, par * 16 * LC:(par + 1) * 16 * LC]
                szk = f'sz{par}'
                bck = f'bc{par}'
                xnp = xn_fm[:, par * 8 * LC:(par + 1) * 8 * LC]
                xnk = f'xn{par}'
                TH(L, 'vector', lambda v: v.tensor_copy(
                    xcfull[:, :].rearrange("p (g e) -> p g e", e=LC + 3)[:, :, 0:3],
                    halo_sb[:, :].rearrange("p (g e) -> p g e", e=3)),
                   reads=['halo'], writes=['xcfull'], war=povl('xcfull'))
                for q in range(8):
                    wq = winq[q % 2]
                    wqn = f'winq{q % 2}'
                    TH(L, 'gpsimd', lambda s, q=q, wq=wq: s.dma_start(
                        wq[:, :].rearrange("p (dc e) -> p dc e", dc=8),
                        d_winb.rearrange("(dc p) e -> p dc e", p=128)[:, :, q * 512:(q + 1) * 512]),
                       reads=['cv_win'], writes=[wqn], dma_sem=f'd_win{q % 2}',
                       war=(povl(wqn) if q < 2 else ()), est=9000)
                    for el in range(4):
                        ec = q * 4 + el
                        ps, pn = P()
                        for dc in range(8):
                            TH(L, 'tensor', lambda te, wq=wq, el=el, dc=dc, ps=ps, xnp=xnp: te.matmul(
                                ps[:, 0:LC],
                                wq[:, dc * 512 + el * 128: dc * 512 + (el + 1) * 128],
                                xnp[:, dc * LC:(dc + 1) * LC],
                                start=(dc == 0), stop=(dc == 7)),
                               reads=[wqn, xnk], writes=[pn], inc=(dc == 7))
                        if ec < 16:
                            TH(L, 'scalar', lambda sc, ec=ec, ps=ps: sc.copy(
                                xcfull[:, ec * (LC + 3) + 3: ec * (LC + 3) + 3 + LC],
                                ps[:, 0:LC]),
                               reads=[pn], writes=['xcfull'])
                        else:
                            TH(L, 'scalar', lambda sc, ec=ec, ps=ps, szp=szp: sc.activation(
                                szp[:, (ec - 16) * LC:(ec - 15) * LC], ps[:, 0:LC], AF.Silu),
                               reads=[pn], writes=[szk], tab='s')
                xdblps = None
                for g_i in range(16):
                    base = g_i * (LC + 3)
                    xr = xcv_r[g_i % 2]
                    xrk = f'xcv{g_i % 2}'
                    xb = xcb_r[g_i % 2]
                    xbk = f'xcb{g_i % 2}'
                    TH(L, 'vector', lambda v, base=base, xr=xr, g_i=g_i: v.tensor_scalar(
                        xr[:, :], xcfull[:, base: base + LC],
                        convw_sb[:, g_i * 4: g_i * 4 + 1], None, OP.mult),
                       reads=['xcfull', 'convw'], writes=[xrk],
                       war=(povl(xrk) if g_i < 2 else ()))
                    for k in range(1, 4):
                        TH(L, 'vector', lambda v, base=base, xr=xr, g_i=g_i, k=k:
                           v.scalar_tensor_tensor(
                               xr[:, :], xcfull[:, base + k: base + k + LC],
                               convw_sb[:, g_i * 4 + k: g_i * 4 + k + 1],
                               xr[:, :], OP.mult, OP.add),
                           reads=['xcfull', 'convw', xrk], writes=[xrk])
                    TH(L, 'scalar', lambda sc, xr=xr, g_i=g_i: sc.activation(
                        xr[:, :], xr[:, :], AF.Silu, bias=convb_sb[:, g_i:g_i + 1]),
                       reads=[xrk, 'convb'], writes=[xrk], tab='s')
                    TH(L, 'scalar', lambda sc, xr=xr, xb=xb: sc.copy(xb[:, :], xr[:, :]),
                       reads=[xrk], writes=[xbk],
                       war=(povl(xbk) if g_i < 2 else ()))
                    TH(L, 'sync', lambda s, xr=xr, c=c, g_i=g_i: s.dma_start(
                        d_xcs[(c * 16 + g_i) * 128:(c * 16 + g_i + 1) * 128, :], xr[:, :]),
                       reads=[xrk], writes=[f'dxc{g_i}'], dma_sem=f'd_xco{g_i % 2}', est=3200)
                    psx, pxn = (xdblps if g_i % 4 != 0 else P())
                    if g_i % 4 == 0:
                        xdblps = (psx, pxn)
                    TH(L, 'tensor', lambda te, g_i=g_i, xb=xb, psx=psx: te.matmul(
                        psx[0:96, 0:LC], wxt_bf[:, g_i * 96:(g_i + 1) * 96],
                        xb[:, :], start=(g_i % 4 == 0), stop=(g_i % 4 == 3)),
                       reads=['wxt', xbk], writes=[pxn], inc=True)
                    if g_i % 4 == 3:
                        if g_i == 3:
                            TH(L, 'vector', lambda v, psx=psx: v.tensor_copy(
                                xdbl_sb[:, :], psx[0:96, 0:LC]),
                               reads=[pxn], writes=['xdbl'])
                        else:
                            TH(L, 'vector', lambda v, psx=psx: v.tensor_add(
                                xdbl_sb[:, :], psx[0:96, 0:LC], xdbl_sb[:, :]),
                               reads=[pxn, 'xdbl'], writes=['xdbl'])
                TH(L, 'vector', lambda v: v.tensor_copy(
                    halo_sb[:, :].rearrange("p (g e) -> p g e", e=3),
                    xcfull[:, :].rearrange("p (g e) -> p g e", e=LC + 3)[:, :, LC:LC + 3]),
                   reads=['xcfull'], writes=['halo'])
                TH(L, 'scalar', lambda sc, par=par: sc.copy(
                    bc_bf[:, par * LC:(par + 1) * LC], xdbl_sb[0:32, :]),
                   reads=['xdbl'], writes=[bck])
                TH(L, 'scalar', lambda sc: sc.copy(dtlow_bf[0:32, :], xdbl_sb[32:64, :]),
                   reads=['xdbl'], writes=['dtlow'])
                TH(L, 'scalar', lambda sc: sc.copy(dtlow_bf[32:64, :], xdbl_sb[64:96, :]),
                   reads=['xdbl'], writes=['dtlow'])
                for g_i in range(16):
                    ps, pn = P()
                    dr = dtr_r[g_i % 2]
                    drk = f'dtr{g_i % 2}'
                    TH(L, 'tensor', lambda te, g_i=g_i, ps=ps: te.matmul(
                        ps[:, 0:LC], wdtt_bf[:, g_i * 128:(g_i + 1) * 128],
                        dtlow_bf[:, :], start=True, stop=True),
                       reads=['wdtt', 'dtlow'], writes=[pn])
                    TH(L, 'scalar', lambda sc, g_i=g_i, ps=ps, dr=dr: sc.activation(
                        dr[:, :], ps[:, 0:LC], AF.Exp,
                        bias=negbdt[:, g_i:g_i + 1]),
                       reads=[pn, 'negbdt'], writes=[drk],
                       war=(povl(drk) if g_i < 2 else ()))
                    TH(L, 'scalar', lambda sc, dr=dr: sc.activation(
                        dr[:, :], dr[:, :], AF.Ln, bias=one_sb[:, 0:1]),
                       reads=[drk, 'one'], writes=[drk])
                    TH(L, 'sync', lambda s, dr=dr, c=c, g_i=g_i: s.dma_start(
                        d_dts[(c * 16 + g_i) * 128:(c * 16 + g_i + 1) * 128, :], dr[:, :]),
                       reads=[drk], writes=[f'ddt{g_i}'], dma_sem=f'd_dto{g_i % 2}', est=3200)
                return L

            def _c_slice_in(c, g_i):
                L = []
                r = g_i % 3
                TH(L, 'sync', lambda s, c=c, g_i=g_i, r=r: s.dma_start(
                    dtin[:, r * LC:(r + 1) * LC],
                    d_dts[(c * 16 + g_i) * 128:(c * 16 + g_i + 1) * 128, :]),
                   reads=[f'ddt{g_i}'], writes=[f'dti{r}'], dma_sem=f'd_dti{r}', est=3200)
                TH(L, 'sync', lambda s, c=c, g_i=g_i, r=r: s.dma_start(
                    xcin[:, r * LC:(r + 1) * LC],
                    d_xcs[(c * 16 + g_i) * 128:(c * 16 + g_i + 1) * 128, :]),
                   reads=[f'dxc{g_i}'], writes=[f'xci{r}'], dma_sem=f'd_xci{r}', est=3200)
                return L

            def phase_C_prologue(c):
                L = []
                par = c % 2
                bck = f'bc{par}'
                bcs = bc_bf[:, par * LC:(par + 1) * LC]
                bo = par * 16 * LC
                for n in range(16):
                    ps, pn = P()
                    TH(L, 'tensor', lambda te, n=n, ps=ps, bcs=bcs: te.matmul(
                        ps[:, 0:LC], sel[:, n * 128:(n + 1) * 128], bcs,
                        start=True, stop=True),
                       reads=['sel', bck], writes=[pn])
                    TH(L, 'scalar', lambda sc, n=n, ps=ps, bo=bo: sc.copy(
                        ball[:, bo + n * LC:bo + (n + 1) * LC], ps[:, 0:LC]),
                       reads=[pn], writes=[f'ball{par}'])
                    ps, pn = P()
                    TH(L, 'tensor', lambda te, n=n, ps=ps, bcs=bcs: te.matmul(
                        ps[:, 0:LC], sel[:, (16 + n) * 128:(17 + n) * 128], bcs,
                        start=True, stop=True),
                       reads=['sel', bck], writes=[pn])
                    TH(L, 'scalar', lambda sc, n=n, ps=ps, bo=bo: sc.copy(
                        call[:, bo + n * LC:bo + (n + 1) * LC], ps[:, 0:LC]),
                       reads=[pn], writes=[f'call{par}'])
                for g_i in range(3):
                    L += _c_slice_in(c, g_i)
                return L

            def _c_gate(c, g_i):
                L = []
                par = c % 2
                szk = f'sz{par}'
                gtk = f'gated{par}'
                gtp = gated[:, par * 16 * LC:(par + 1) * 16 * LC]
                psy, psyk = psY_p[g_i % 2]
                TH(L, 'vector', lambda v, g_i=g_i, gtp=gtp, par=par, psy=psy: v.tensor_mul(
                    gtp[:, g_i * LC:(g_i + 1) * LC], psy[:, 0:LC],
                    sz_bf[:, (par * 16 + g_i) * LC:(par * 16 + g_i + 1) * LC]),
                   reads=[psyk, szk], writes=[gtk])
                return L

            def phase_C_iter(c, g_i):
                L = []
                par = c % 2
                r = g_i % 3
                r2 = g_i % 2
                szk = f'sz{par}'
                gtk = f'gated{par}'
                gtp = gated[:, par * 16 * LC:(par + 1) * 16 * LC]
                dts = dtin[:, r * LC:(r + 1) * LC]
                xcs = xcin[:, r * LC:(r + 1) * LC]
                us = u_r[:, r2 * LC:(r2 + 1) * LC]
                xbs = xcbf_r[:, r2 * LC:(r2 + 1) * LC]
                TH(L, 'vector', lambda v, dts=dts, xcs=xcs, us=us: v.tensor_mul(
                    us, dts, xcs),
                   reads=[f'dti{r}', f'xci{r}'], writes=[f'u{r2}'])
                TH(L, 'scalar', lambda sc, xcs=xcs, xbs=xbs: sc.copy(xbs, xcs),
                   reads=[f'xci{r}'], writes=[f'xcb2{r2}'])
                for half in range(2):
                    p_i = (g_i * 2 + half) % 2
                    da = da_p[p_i]
                    dbx = dbx_p[p_i]
                    hc = hc_p[p_i]
                    dak = f'da{p_i}'
                    dbk = f'dbx{p_i}'
                    TH(L, 'gpsimd', lambda g, g_i=g_i, half=half, dbx=dbx: g.tensor_copy(
                        dbx[:, 0::LC + 1],
                        hcarry[:, g_i * 16 + half * NH: g_i * 16 + half * NH + NH]),
                       reads=['hcarry'], writes=[dbk], est=700)
                    for i in range(NH):
                        n = half * NH + i
                        TH(L, 'scalar', lambda sc, g_i=g_i, i=i, n=n, da=da, dts=dts: sc.activation(
                            da[:, i * (LC + 1) + 1:(i + 1) * (LC + 1)], dts, AF.Exp,
                            scale=a_pos[:, g_i * 16 + n: g_i * 16 + n + 1]),
                           reads=[f'dti{r}', 'a_pos'], writes=[dak], est=650)
                    for i in range(NH):
                        n = half * NH + i
                        TH(L, 'vector', lambda v, i=i, n=n, dbx=dbx, us=us, par=par: v.tensor_tensor(
                            dbx[:, i * (LC + 1) + 1:(i + 1) * (LC + 1)], us,
                            ball[:, (par * 16 + n) * LC:(par * 16 + n + 1) * LC], OP.mult),
                           reads=[f'u{r2}', f'ball{par}'], writes=[dbk], est=450)
                    TH(L, 'vector', lambda v, da=da, dbx=dbx: v.tensor_tensor_scan(
                        dbx[:, 0:NH * (LC + 1)],
                        da[:, 0:NH * (LC + 1)],
                        dbx[:, 0:NH * (LC + 1)],
                        0.0, OP.mult, OP.add),
                       reads=[dak, dbk], writes=[dbk], est=4600)
                    TH(L, 'gpsimd', lambda g, g_i=g_i, half=half, dbx=dbx: g.tensor_copy(
                        hcarry[:, g_i * 16 + half * NH: g_i * 16 + half * NH + NH],
                        dbx[:, LC::LC + 1]),
                       reads=[dbk], writes=['hcarry'], est=700)
                    TH(L, 'gpsimd', lambda g, half=half, dbx=dbx, hc=hc, par=par: g.tensor_tensor(
                        hc[:, :].rearrange("p (n l) -> p n l", n=NH),
                        dbx[:, :].rearrange("p (n l) -> p n l", l=LC + 1)[:, :, 1:LC + 1],
                        call[:, (par * 16 + half * NH) * LC:(par * 16 + (half + 1) * NH) * LC]
                        .rearrange("p (n l) -> p n l", n=NH), OP.mult),
                       reads=[dbk, f'call{par}'], writes=[dbk], est=4300)
                    psy, psyk = psY_p[g_i % 2]
                    for i in range(NH):
                        TH(L, 'tensor', lambda te, i=i, half=half, hc=hc, psy=psy: te.matmul(
                            psy[:, 0:LC], identb[:, :], hc[:, i * LC:(i + 1) * LC],
                            start=(half == 0 and i == 0), stop=False),
                           reads=[dbk, 'identb'], writes=[psyk], inc=(i == NH - 1))
                psy, psyk = psY_p[g_i % 2]
                TH(L, 'tensor', lambda te, g_i=g_i, xbs=xbs, psy=psy: te.matmul(
                    psy[:, 0:LC], diagdp[:, g_i * 128:(g_i + 1) * 128],
                    xbs, start=False, stop=True),
                   reads=['diagdp', f'xcb2{r2}'], writes=[psyk])
                if g_i >= 1:
                    L += _c_gate(c, g_i - 1)
                if g_i == 15:
                    L += _c_gate(c, 15)
                if g_i + 3 < 16:
                    L += _c_slice_in(c, g_i + 3)
                return L

            def phase_D(c):
                L = []
                par = c % 2
                gtk = f'gated{par}'
                gtp = gated[:, par * 16 * LC:(par + 1) * 16 * LC]
                TH(L, 'gpsimd', lambda s: s.dma_start(
                    woutb[:, :].rearrange("p (gi e) -> p gi e", gi=16),
                    d_woutb.rearrange("(gi p) e -> p gi e", p=128)),
                   reads=['cv_wout'], writes=['woutb'], dma_sem='d_wout', war=povl('woutb'), est=26000)
                for m in range(8):
                    ps, pn = P()
                    for g_i in range(16):
                        TH(L, 'tensor', lambda te, m=m, g_i=g_i, ps=ps, gtp=gtp: te.matmul(
                            ps[:, 0:LC],
                            woutb[:, g_i * 1024 + m * 128: g_i * 1024 + (m + 1) * 128],
                            gtp[:, g_i * LC:(g_i + 1) * LC],
                            start=(g_i == 0), stop=(g_i == 15)),
                           reads=['woutb', gtk], writes=[pn], inc=(g_i == 15))
                    TH(L, 'scalar', lambda sc, m=m, ps=ps: sc.copy(
                        s_fm[:, m * LC:(m + 1) * LC], ps[:, 0:LC]),
                       reads=[pn], writes=['s_fm'])
                return L

            def phase_E(c):
                L = []
                par = c % 2
                exp_i = 0 if c * LC < SEQ // 2 else 1
                xnk, ln2k = f'xn{par}', f'ln2{par}'
                xnp = xn_fm[:, par * 8 * LC:(par + 1) * 8 * LC]
                ln2p = ln2_fmb[:, par * 8 * LC:(par + 1) * 8 * LC]

                def w1_load(fb):
                    TH(L, 'gpsimd', lambda s, fb=fb, exp_i=exp_i: s.dma_start(
                        w1r[fb % 2][:, :].rearrange("p (dc f) -> p dc f", dc=8),
                        d_w1b[exp_i].rearrange("(dc p) f -> p dc f", p=128)
                        [:, :, fb * 512:(fb + 1) * 512]),
                       reads=['cv_w1'], writes=[f'w1b{fb % 2}'], dma_sem=f'd_w1{fb % 2}',
                       war=(povl(f'w1b{fb % 2}') if fb < 2 else ()), est=9000)

                def w2_load(fb):
                    TH(L, 'gpsimd', lambda s, fb=fb, exp_i=exp_i: s.dma_start(
                        w2r[fb % 2][:, :].rearrange("p (fp e) -> p fp e", fp=4),
                        d_w2b[exp_i].rearrange("(fq p) e -> p fq e", p=128)
                        [:, fb * 4:(fb + 1) * 4, :]),
                       reads=['cv_w2'], writes=[f'w2b{fb % 2}'], dma_sem=f'd_w2{fb % 2}',
                       war=(povl(f'w2b{fb % 2}') if fb < 2 else ()), est=9000)

                for m in range(8):
                    ar = attn_r[m % 2]
                    ark = f'at{m % 2}'
                    TH(L, 'gpsimd', lambda s, m=m, ar=ar: s.dma_start(
                        ar[:, :].rearrange("p (dc e) -> p dc e", dc=8),
                        d_attnbf.rearrange("(dc p) e -> p dc e", p=128)
                        [:, :, m * 128:(m + 1) * 128]),
                       reads=['cv_at'], writes=[ark], dma_sem=f'd_at{m % 2}',
                       war=(povl(ark) if m < 2 else ()))
                    ps, pn = P()
                    for dc in range(8):
                        TH(L, 'tensor', lambda te, m=m, dc=dc, ps=ps, ar=ar, ln2p=ln2p: te.matmul(
                            ps[:, 0:LC], ar[:, dc * 128:(dc + 1) * 128],
                            ln2p[:, dc * LC:(dc + 1) * LC],
                            start=(dc == 0), stop=(dc == 7)),
                           reads=[ark, ln2k], writes=[pn], inc=(dc == 7))
                    TH(L, 'vector', lambda v, m=m, ps=ps, xnp=xnp: v.scalar_tensor_tensor(
                        xa_bf[:, m * LC:(m + 1) * LC], ps[:, 0:LC],
                        biasp[:, m:m + 1], xnp[:, m * LC:(m + 1) * LC],
                        OP.add, OP.add),
                       reads=[pn, 'biasp', xnk], writes=['xa'],
                       war=(povl('xa') if m == 0 else ()))
                w1_load(0)
                w1_load(1)
                w2_load(0)
                for fb in range(8):
                    w1b = w1r[fb % 2]
                    w1k = f'w1b{fb % 2}'
                    w2b = w2r[fb % 2]
                    w2k = f'w2b{fb % 2}'
                    h1b = h1r[fb % 2]
                    h1k = f'h1{fb % 2}'
                    if fb + 1 < 8:
                        w2_load(fb + 1)
                    for fc in range(4):
                        ps, pn = P()
                        for dc in range(8):
                            TH(L, 'tensor', lambda te, fc=fc, dc=dc, ps=ps, w1b=w1b: te.matmul(
                                ps[:, 0:LC],
                                w1b[:, dc * 512 + fc * 128: dc * 512 + (fc + 1) * 128],
                                xa_bf[:, dc * LC:(dc + 1) * LC],
                                start=(dc == 0), stop=(dc == 7)),
                               reads=[w1k, 'xa'], writes=[pn], inc=(dc == 7))
                        TH(L, 'scalar', lambda sc, fb=fb, fc=fc, ps=ps, exp_i=exp_i, h1b=h1b: sc.activation(
                            h1b[:, fc * LC:(fc + 1) * LC], ps[:, 0:LC], AF.Gelu,
                            bias=b1_sb[:, exp_i * 32 + fb * 4 + fc: exp_i * 32 + fb * 4 + fc + 1]),
                           reads=[pn, 'b1'], writes=[h1k],
                           war=(povl(h1k) if fb < 2 and fc == 0 else ()), tab='g')
                    if fb + 2 < 8:
                        w1_load(fb + 2)
                    for m in range(8):
                        for fp in range(4):
                            TH(L, 'tensor', lambda te, fb=fb, m=m, fp=fp, w2b=w2b, h1b=h1b: te.matmul(
                                psY2[:, m * LC:(m + 1) * LC],
                                w2b[:, fp * 1024 + m * 128: fp * 1024 + (m + 1) * 128],
                                h1b[:, fp * LC:(fp + 1) * LC],
                                start=(fb == 0 and fp == 0 and m % 2 == 0),
                                stop=(fb == 7 and fp == 3 and m % 2 == 1)),
                               reads=[w2k, h1k], writes=['psY2'],
                               inc=(m == 7 and fp == 3))
                for m in range(8):
                    TH(L, 'scalar', lambda sc, m=m, exp_i=exp_i: sc.activation(
                        y2_bf[:, m * LC:(m + 1) * LC], psY2[:, m * LC:(m + 1) * LC],
                        AF.Identity, bias=b2_sb[:, exp_i * 8 + m: exp_i * 8 + m + 1]),
                       reads=['psY2', 'b2'], writes=['y2'],
                       war=(povl('y2') if m == 0 else ()))
                for m in range(8):
                    nr = next_r[m % 2]
                    nrk = f'at{m % 2}'
                    TH(L, 'gpsimd', lambda s, m=m, nr=nr: s.dma_start(
                        nr[:, :].rearrange("p (dc e) -> p dc e", dc=8),
                        d_nxwbf.rearrange("(dc p) e -> p dc e", p=128)
                        [:, :, m * 128:(m + 1) * 128]),
                       reads=['cv_nx'], writes=[nrk], dma_sem=f'd_nx{m % 2}',
                       war=(povl(nrk) if m < 2 else ()))
                    ps, pn = P()
                    for dc in range(8):
                        TH(L, 'tensor', lambda te, m=m, dc=dc, ps=ps, nr=nr: te.matmul(
                            ps[:, 0:LC], nr[:, dc * 128:(dc + 1) * 128],
                            y2_bf[:, dc * LC:(dc + 1) * LC],
                            start=(dc == 0), stop=(dc == 7)),
                           reads=[nrk, 'y2'], writes=[pn], inc=(dc == 7))
                    TH(L, 'scalar', lambda sc, m=m, ps=ps: sc.activation(
                        gg[:, m * LC:(m + 1) * LC], ps[:, 0:LC], AF.Gelu,
                        bias=nextb_sb[:, m:m + 1]),
                       reads=[pn, 'nextb'], writes=['gg'],
                       war=(povl('gg') if m == 0 else ()), tab='g')
                TH(L, 'scalar', lambda sc: sc.activation(gg[:, :], gg[:, :], AF.Silu),
                   reads=['gg'], writes=['gg'], tab='s')
                return L

            def phase_F(c):
                L = []
                t0 = c * LC
                TH(L, 'sync', lambda s, t0=t0: s.dma_start(
                    xre[:, :].rearrange("p (tt d) -> p tt d", tt=NTT),
                    dx[t0:t0 + LC, :].rearrange("(tt p) d -> p tt d", p=128)),
                   writes=['xre'], dma_sem='d_xre', war=povl('xre'), est=7000)
                TH(L, 'vector', lambda v: v.tensor_mul(comb[:, :], s_fm[:, :], gg[:, :]),
                   reads=['s_fm', 'gg'], writes=['comb'], war=povl('comb'), est=2400)
                for i, (m, tt) in enumerate([(m, tt) for tt in range(NTT) for m in range(8)]):
                    ps, pn = P()
                    TH(L, 'tensor', lambda te, m=m, tt=tt, ps=ps: te.transpose(
                        ps[:, 0:128], comb[:, m * LC + tt * 128: m * LC + tt * 128 + 128],
                        identf[:, :]),
                       reads=['comb', 'identf'], writes=[pn])
                    TH(L, 'vector', lambda v, m=m, tt=tt, ps=ps: v.tensor_add(
                        out_tm[:, tt * D + m * 128: tt * D + (m + 1) * 128],
                        ps[:, 0:128],
                        xre[:, tt * D + m * 128: tt * D + (m + 1) * 128]),
                       reads=[pn, 'xre'], writes=['out_tm'],
                       war=(povl('out_tm') if i == 0 else ()))
                TH(L, 'sync', lambda s, t0=t0: s.dma_start(
                    d_out[t0:t0 + LC, :].rearrange("(tt p) d -> p tt d", p=128),
                    out_tm[:, :].rearrange("p (tt d) -> p tt d", tt=NTT)),
                   reads=['out_tm'], dma_sem='d_xout', est=7000)
                return L

            def emit_list(L):
                for eng, build, kw in L:
                    E(eng, build, **kw)

            prog = []
            TH(prog, 'gpsimd', lambda g: g.dma_start(d_woutb, d_woutt),
               writes=['cv_wout'], dma_sem='d_cv_wout', est=12000)
            for i in range(2):
                TH(prog, 'gpsimd', lambda g, i=i: g.dma_start(d_w1b[i], d_w1[i]),
                   writes=['cv_w1'], dma_sem='d_cv_w1', est=30000)
                TH(prog, 'gpsimd', lambda g, i=i: g.dma_start(d_w2b[i], d_w2[i]),
                   writes=['cv_w2'], dma_sem='d_cv_w2', est=30000)
            TH(prog, 'gpsimd', lambda g: g.dma_start(d_nxwbf, d_nextw),
               writes=['cv_nx'], dma_sem='d_cv_nx', est=12000)
            TH(prog, 'vector', lambda v: v.memset(hcarry[:, :], 0.0), writes=['hcarry'])
            TH(prog, 'vector', lambda v: v.memset(halo_sb[:, :], 0.0), writes=['halo'])
            TH(prog, 'scalar', lambda sc: sc.memzero(da0[:, 0::LC + 1]), writes=['da0'])
            TH(prog, 'scalar', lambda sc: sc.memzero(da1[:, 0::LC + 1]), writes=['da1'])
            prog += phase_A(0)
            prog += phase_B(0)

            def stamp(L, t):
                for eng, build, kw in L:
                    kw['slot'] = t
                return L

            for s in range(NCHUNK + 1):
                bg = []
                if s >= 1:
                    bg += phase_D(s - 1) + phase_E(s - 1) + phase_F(s - 1)
                if s + 1 < NCHUNK:
                    bg += phase_A(s + 1) + phase_B(s + 1)
                stamp(bg, s)
                if s < NCHUNK:
                    prog += stamp(phase_C_prologue(s), s)
                    nb = max(1, (len(bg) + 15) // 16)
                    for g_i in range(16):
                        prog += stamp(phase_C_iter(s, g_i), s)
                        prog += bg[g_i * nb:(g_i + 1) * nb]
                    prog += bg[16 * nb:]
                else:
                    prog += bg
            emit_list(schedule_ops(prog))

            final_waits = [(dn, em.cnt[dn]) for dn in DMA_SEMS if em.cnt[dn] > 0]
            em.ops.append(('sync', lambda s: None, final_waits, None, None))

            with nc.Block() as blk1:
                @blk1.sync
                def _(s):
                    em.run('sync', s, sems)

                @blk1.scalar
                def _(s):
                    em.run('scalar', s, sems)

                @blk1.vector
                def _(s):
                    em.run('vector', s, sems)

                @blk1.tensor
                def _(s):
                    em.run('tensor', s, sems)

                @blk1.gpsimd
                def _(s):
                    em.run('gpsimd', s, sems)

    return nc


def _layout_inputs(inputs):
    A = np.ascontiguousarray
    x = inputs['L0_attn_x']
    W_inT = A(inputs['L0_attn_W_in'].T)
    perm = list(range(64, 80)) + list(range(80, 96)) + list(range(0, 64))
    W_xTp = A(inputs['L0_attn_W_x'][perm].T)
    W_dtT = A(inputs['L0_attn_W_dt'].T)
    convw = A(inputs['L0_attn_conv_w'].reshape(16, 128, 4).transpose(1, 0, 2).reshape(128, 64))
    convb = A(inputs['L0_attn_conv_b'].reshape(16, 128).T)
    b_dt = A(inputs['L0_attn_b_dt'].reshape(16, 128).T)
    A_log = A(inputs['L0_attn_A_log'].reshape(16, 128, 16).transpose(1, 0, 2).reshape(128, 256))
    Dp = A(inputs['L0_attn_Dp'].reshape(16, 128).T)
    W_outT = A(inputs['L0_attn_W_out'].T)
    ln_g = A(inputs['L0_ffn_ln_g'].reshape(8, 128).T)
    ln_b = A(inputs['L0_ffn_ln_b'].reshape(8, 128).T)
    attn_w = A(inputs['L0_ffn_attn_w'])
    attn_b = A(inputs['L0_ffn_attn_b'].reshape(8, 128).T)
    w1 = inputs['L0_ffn_w1']
    b1 = inputs['L0_ffn_b1']
    w2 = inputs['L0_ffn_w2']
    b2 = inputs['L0_ffn_b2']
    next_w = A(inputs['L0_ffn_next_w'])
    next_b = A(inputs['L0_ffn_next_b'].reshape(8, 128).T)

    in_maps = []
    for b in range(B_FULL):
        e0 = 2 * (b % 4)
        e1 = e0 + 1
        in_maps.append({
            "x": A(x[b]),
            "W_inT": W_inT,
            "convw": convw,
            "convb": convb,
            "W_xTp": W_xTp,
            "W_dtT": W_dtT,
            "b_dt": b_dt,
            "A_log": A_log,
            "Dp": Dp,
            "W_outT": W_outT,
            "ln_g": ln_g,
            "ln_b": ln_b,
            "attn_w": attn_w,
            "attn_b": attn_b,
            "w1A": A(w1[e0]),
            "w1B": A(w1[e1]),
            "b1A": A(b1[e0].reshape(32, 128).T),
            "b1B": A(b1[e1].reshape(32, 128).T),
            "w2A": A(w2[e0]),
            "w2B": A(w2[e1]),
            "b2A": A(b2[e0].reshape(8, 128).T),
            "b2B": A(b2[e1].reshape(8, 128).T),
            "next_w": next_w,
            "next_b": next_b,
        })
    return in_maps


def kernel(**inputs):
    inputs = {k: np.asarray(v, dtype=np.float32) for k, v in inputs.items()}
    in_maps = _layout_inputs(inputs)
    nc = build_nc()
    res = run_bass_kernel_spmd(nc, in_maps, core_ids=list(range(B_FULL)))
    out = np.stack([res.results[b]["out"] for b in range(B_FULL)], axis=0)
    return out


# version 30
# speedup vs baseline: 1.0151x; 1.0151x over previous
import contextlib
import numpy as np
import concourse.bass as bass
import concourse.mybir as mybir
from concourse.alu_op_type import AluOpType
from concourse.bass_utils import run_bass_kernel_spmd

F32 = mybir.dt.float32
BF16 = mybir.dt.bfloat16
AF = mybir.ActivationFunctionType
OP = AluOpType

B_FULL = 8
SEQ = 4096
D = 1024
DI = 2048
E2 = 4096
DFF = 4096
NST = 16
DTR = 64
LC = 256
NCHUNK = SEQ // LC
NTT = LC // 128
NH = 8
LN_EPS = 1e-5

DMA_SEMS = ('d_su_s', 'd_su_g', 'd_cv_win', 'd_cv_wout', 'd_cv_w1', 'd_cv_w2', 'd_cv_nx', 'd_xin', 'd_xre', 'd_xout',
            'd_win0', 'd_win1', 'd_wout', 'd_w10', 'd_w11', 'd_w20', 'd_w21',
            'd_at0', 'd_at1', 'd_nx0', 'd_nx1',
            'd_dto0', 'd_dto1', 'd_xco0', 'd_xco1',
            'd_dti0', 'd_dti1', 'd_dti2', 'd_xci0', 'd_xci1', 'd_xci2')


class Emitter:

    def __init__(self):
        self.ops = []
        self.cnt = {}
        self.writer = {}
        self.readers = {}
        self.pending_reads = {}
        self.pending_writes = {}
        self.watermark = {}
        for e in ('sync', 'scalar', 'vector', 'tensor', 'gpsimd'):
            self.cnt['e_' + e] = 0
            self.pending_reads[e] = []
            self.pending_writes[e] = []
        for dn in DMA_SEMS:
            self.cnt[dn] = 0

    def _deps(self, reads, writes):
        deps = {}
        for b in reads:
            t = self.writer.get(b)
            if t:
                deps[t[0]] = max(deps.get(t[0], 0), t[1])
        for b in writes:
            t = self.writer.get(b)
            if t:
                deps[t[0]] = max(deps.get(t[0], 0), t[1])
            for t in self.readers.get(b, ()):
                deps[t[0]] = max(deps.get(t[0], 0), t[1])
        return deps

    def emit(self, eng, build, reads=(), writes=(), inc=True, dma_sem=None, war=(), est=None, tab=None, slot=None):
        reads = list(reads)
        writes = list(writes)
        deps = self._deps(reads, list(writes) + list(war))
        waits = []
        for s, v in deps.items():
            if s.startswith('d_'):
                v = self.cnt[s]
            if self.watermark.get((eng, s), -1) < v:
                waits.append((s, v))
                self.watermark[(eng, s)] = v
        if dma_sem is not None:
            self.cnt[dma_sem] += 16
            token = (dma_sem, self.cnt[dma_sem])
        elif inc:
            self.cnt['e_' + eng] += 1
            token = ('e_' + eng, self.cnt['e_' + eng])
        else:
            token = None

        if token is not None:
            reads += self.pending_reads[eng]
            writes_all = writes + self.pending_writes[eng]
            self.pending_reads[eng] = []
            self.pending_writes[eng] = []
            for b in reads:
                self.readers.setdefault(b, set()).add(token)
            for b in writes_all:
                self.writer[b] = token
                self.readers[b] = set()
        else:
            self.pending_reads[eng] += reads
            self.pending_writes[eng] += writes

        self.ops.append((eng, build, waits, token, dma_sem))
        return token

    def fence_all(self):
        waits = [(k, v) for k, v in self.cnt.items() if v > 0]
        for e in ('sync', 'scalar', 'vector', 'tensor', 'gpsimd'):
            mywaits = []
            for s, v in waits:
                if self.watermark.get((e, s), -1) < v:
                    mywaits.append((s, v))
                    self.watermark[(e, s)] = v
            self.ops.append((e, lambda s: None, mywaits, None, None))

    def run(self, eng_name, section, sems):
        for e, build, waits, token, dma_sem in self.ops:
            if e != eng_name:
                continue
            for s, v in waits:
                section.wait_ge(sems[s], v)
            ins = build(section)
            if token is not None and ins is not None:
                if dma_sem is not None:
                    ins.then_inc(sems[dma_sem], 16)
                else:
                    ins.then_inc(sems[token[0]], 1)


def TH(lst, eng, build, **kw):
    lst.append((eng, build, kw))


ENGS = ('sync', 'scalar', 'vector', 'tensor', 'gpsimd')
DEF_EST = {'sync': 150, 'scalar': 520, 'vector': 520, 'tensor': 190, 'gpsimd': 1100}


def schedule_ops(prog):
    n = len(prog)
    deps = [set() for _ in range(n)]
    writer = {}
    readers = {}
    pend_r = {e: [] for e in ENGS}
    pend_w = {e: [] for e in ENGS}
    pend_idx = {e: [] for e in ENGS}
    for idx, (eng, build, kw) in enumerate(prog):
        reads = list(kw.get('reads', ()))
        writes = list(kw.get('writes', ()))
        war = list(kw.get('war', ()))
        dma = kw.get('dma_sem')
        inc = kw.get('inc', True)
        for b in reads:
            w = writer.get(b)
            if w is not None:
                deps[idx].add(w)
        for b in writes + war:
            w = writer.get(b)
            if w is not None:
                deps[idx].add(w)
            for rd in readers.get(b, ()):
                deps[idx].add(rd)
        if dma is not None or inc:
            allr = reads + pend_r[eng]
            allw = writes + pend_w[eng]
            for j in pend_idx[eng]:
                deps[idx].add(j)
            pend_r[eng] = []
            pend_w[eng] = []
            pend_idx[eng] = []
            for b in allr:
                readers.setdefault(b, set()).add(idx)
            for b in allw:
                writer[b] = idx
                readers[b] = set()
        else:
            pend_r[eng] += reads
            pend_w[eng] += writes
            pend_idx[eng].append(idx)
        deps[idx].discard(idx)

    unit_of = [None] * n
    units = []
    openu = {e: None for e in ENGS}
    for idx, (eng, build, kw) in enumerate(prog):
        inc = kw.get('inc', True)
        dma = kw.get('dma_sem')
        if openu[eng] is not None:
            u = openu[eng]
        else:
            u = len(units)
            units.append([])
            openu[eng] = u
        units[u].append(idx)
        unit_of[idx] = u
        if inc or dma is not None:
            openu[eng] = None
    nu = len(units)
    udeps = [set() for _ in range(nu)]
    for idx in range(n):
        u = unit_of[idx]
        for d in deps[idx]:
            du = unit_of[d]
            if du != u:
                udeps[u].add(du)
    ueng = [prog[units[u][0]][0] for u in range(nu)]
    utab = []
    for u in range(nu):
        t = None
        if ueng[u] == 'scalar':
            t = 'e'
            for idx in units[u]:
                t2 = prog[idx][2].get('tab')
                if t2:
                    t = t2
        utab.append(t)
    udur = []
    uisdma = []
    for u in range(nu):
        d = 0
        isdma = False
        for idx in units[u]:
            kw = prog[idx][2]
            if kw.get('dma_sem') is not None:
                isdma = True
                d += kw.get('est', 5000)
            else:
                d += kw.get('est', DEF_EST[prog[idx][0]])
        udur.append(d)
        uisdma.append(isdma)
    succ = [[] for _ in range(nu)]
    for u in range(nu):
        for d in udeps[u]:
            succ[d].append(u)
    height = [0] * nu
    for u in range(nu - 1, -1, -1):
        h = 0
        for s2 in succ[u]:
            if height[s2] > h:
                h = height[s2]
        height[u] = h + udur[u]
    maxslot = 0
    uslot = []
    for u in range(nu):
        t = None
        for idx in units[u]:
            t2 = prog[idx][2].get('slot')
            if t2 is not None:
                t = t2 if t is None else min(t, t2)
        uslot.append(t)
        if t is not None and t > maxslot:
            maxslot = t
    WAGE = 32000
    for u in range(nu):
        if uslot[u] is not None and ueng[u] != 'vector':
            height[u] += WAGE * (maxslot - uslot[u])
    ndeps = [len(udeps[u]) for u in range(nu)]
    done_t = [0.0] * nu
    ready_t = [0.0] * nu
    avail = {e: 0.0 for e in ENGS}
    remaining = {e: 0 for e in ENGS}
    for u in range(nu):
        remaining[ueng[u]] += 1
    ready = {e: [] for e in ENGS}
    for u in range(nu):
        if ndeps[u] == 0:
            ready[ueng[u]].append(u)
    order = []
    scheduled = [False] * nu
    cur_tab = ['e']
    INF = float('inf')
    while len(order) < nu:
        e = None
        for e2 in ENGS:
            if remaining[e2] and ready[e2]:
                if e is None or avail[e2] < avail[e]:
                    e = e2
        if e is None:
            best = None
            for e2 in ENGS:
                for u in ready[e2]:
                    if best is None or ready_t[u] < ready_t[best]:
                        best = u
            assert best is None
            break
        now = avail[e]
        cands = [u for u in ready[e] if ready_t[u] <= now]
        if cands:
            if e == 'scalar':
                u = max(cands, key=lambda x: height[x]
                        + (2600 if utab[x] == cur_tab[0] else 0))
            else:
                u = max(cands, key=lambda x: height[x])
        else:
            u = min(ready[e], key=lambda x: (ready_t[x], -height[x]))
        ready[e].remove(u)
        scheduled[u] = True
        remaining[e] -= 1
        start = max(avail[e], ready_t[u])
        dur = udur[u]
        if e == 'scalar' and utab[u] is not None:
            if utab[u] != cur_tab[0]:
                dur += 1300
            cur_tab[0] = utab[u]
        if uisdma[u]:
            avail[e] = start + 200
        else:
            avail[e] = start + dur
        done_t[u] = start + dur
        order.append(u)
        for s2 in succ[u]:
            ndeps[s2] -= 1
            if done_t[u] > ready_t[s2]:
                ready_t[s2] = done_t[u]
            if ndeps[s2] == 0:
                ready[ueng[s2]].append(s2)
    assert len(order) == nu, (len(order), nu)
    out = []
    for u in order:
        for idx in units[u]:
            out.append(prog[idx])
    return out



def build_nc():
    nc = bass.Bass('TRN2', target_bir_lowering=False)

    dx = nc.dram_tensor("x", [SEQ, D], F32, kind="ExternalInput").ap()
    d_wint = nc.dram_tensor("W_inT", [D, E2], F32, kind="ExternalInput").ap()
    d_convw = nc.dram_tensor("convw", [128, 64], F32, kind="ExternalInput").ap()
    d_convb = nc.dram_tensor("convb", [128, 16], F32, kind="ExternalInput").ap()
    d_wxt = nc.dram_tensor("W_xTp", [DI, 96], F32, kind="ExternalInput").ap()
    d_wdtt = nc.dram_tensor("W_dtT", [DTR, DI], F32, kind="ExternalInput").ap()
    d_bdt = nc.dram_tensor("b_dt", [128, 16], F32, kind="ExternalInput").ap()
    d_alog = nc.dram_tensor("A_log", [128, 256], F32, kind="ExternalInput").ap()
    d_dp = nc.dram_tensor("Dp", [128, 16], F32, kind="ExternalInput").ap()
    d_woutt = nc.dram_tensor("W_outT", [DI, D], F32, kind="ExternalInput").ap()
    d_lng = nc.dram_tensor("ln_g", [128, 8], F32, kind="ExternalInput").ap()
    d_lnb = nc.dram_tensor("ln_b", [128, 8], F32, kind="ExternalInput").ap()
    d_attnw = nc.dram_tensor("attn_w", [D, D], F32, kind="ExternalInput").ap()
    d_attnb = nc.dram_tensor("attn_b", [128, 8], F32, kind="ExternalInput").ap()
    d_w1 = [nc.dram_tensor("w1A", [D, DFF], F32, kind="ExternalInput").ap(),
            nc.dram_tensor("w1B", [D, DFF], F32, kind="ExternalInput").ap()]
    d_b1 = [nc.dram_tensor("b1A", [128, 32], F32, kind="ExternalInput").ap(),
            nc.dram_tensor("b1B", [128, 32], F32, kind="ExternalInput").ap()]
    d_w2 = [nc.dram_tensor("w2A", [DFF, D], F32, kind="ExternalInput").ap(),
            nc.dram_tensor("w2B", [DFF, D], F32, kind="ExternalInput").ap()]
    d_b2 = [nc.dram_tensor("b2A", [128, 8], F32, kind="ExternalInput").ap(),
            nc.dram_tensor("b2B", [128, 8], F32, kind="ExternalInput").ap()]
    d_nextw = nc.dram_tensor("next_w", [D, D], F32, kind="ExternalInput").ap()
    d_nextb = nc.dram_tensor("next_b", [128, 8], F32, kind="ExternalInput").ap()
    d_out = nc.dram_tensor("out", [SEQ, D], F32, kind="ExternalOutput").ap()

    d_dts = nc.dram_tensor("d_dts", [NCHUNK * 16 * 128, LC], F32).ap()
    d_xcs = nc.dram_tensor("d_xcs", [NCHUNK * 16 * 128, LC], F32).ap()
    d_winb = nc.dram_tensor("d_winb", [D, E2], BF16).ap()
    d_woutb = nc.dram_tensor("d_woutb", [DI, D], BF16).ap()
    d_w1b = [nc.dram_tensor("d_w1bA", [D, DFF], BF16).ap(),
             nc.dram_tensor("d_w1bB", [D, DFF], BF16).ap()]
    d_w2b = [nc.dram_tensor("d_w2bA", [DFF, D], BF16).ap(),
             nc.dram_tensor("d_w2bB", [DFF, D], BF16).ap()]
    d_attnbf = nc.dram_tensor("d_attnbf", [D, D], BF16).ap()
    d_nxwbf = nc.dram_tensor("d_nxwbf", [D, D], BF16).ap()

    em = Emitter()
    E = em.emit

    with contextlib.ExitStack() as _st:
        _A = _st.enter_context
        wxt_bf = _A(nc.sbuf_tensor("wxt_bf", [128, 16 * 96], BF16))
        wdtt_bf = _A(nc.sbuf_tensor("wdtt_bf", [DTR, DI], BF16))
        a_pos = _A(nc.sbuf_tensor("a_pos", [128, 256], F32))
        sel = _A(nc.sbuf_tensor("sel", [32, 32 * 128], BF16))
        identf = _A(nc.sbuf_tensor("identf", [128, 128], F32))
        identb = _A(nc.sbuf_tensor("identb", [128, 128], BF16))
        diagdp = _A(nc.sbuf_tensor("diagdp", [128, 16 * 128], BF16))
        convw_sb = _A(nc.sbuf_tensor("convw_sb", [128, 64], F32))
        dp_sb = _A(nc.sbuf_tensor("dp_sb", [128, 16], F32))
        negbdt = _A(nc.sbuf_tensor("negbdt", [128, 16], F32))
        convb_sb = _A(nc.sbuf_tensor("convb_sb", [128, 16], F32))
        biasp = _A(nc.sbuf_tensor("biasp", [128, 8], F32))
        nextb_sb = _A(nc.sbuf_tensor("nextb_sb", [128, 8], F32))
        b1_sb = _A(nc.sbuf_tensor("b1_sb", [128, 64], F32))
        b2_sb = _A(nc.sbuf_tensor("b2_sb", [128, 16], F32))
        lng_sb = _A(nc.sbuf_tensor("lng_sb", [128, 8], F32))
        ones_bf = _A(nc.sbuf_tensor("ones_bf", [128, 128], BF16))
        eps_sb = _A(nc.sbuf_tensor("eps_sb", [128, 1], F32))
        one_sb = _A(nc.sbuf_tensor("one_sb", [128, 1], F32))
        q_eng = {('e_' + e): _A(nc.semaphore('e_' + e))
                 for e in ('sync', 'scalar', 'vector', 'tensor', 'gpsimd')}
        dma_q = {dn: _A(nc.semaphore(dn)) for dn in DMA_SEMS}
        sems = dict(q_eng)
        sems.update(dma_q)

        with contextlib.ExitStack() as _st0:
            stage = _st0.enter_context(nc.sbuf_tensor("stage", [128, 8 * D], F32))
            attn_st = _st0.enter_context(nc.sbuf_tensor("attn_st", [128, 8 * D], BF16))
            lnb_bf = _st0.enter_context(nc.sbuf_tensor("lnb_bf", [128, 8], BF16))
            attnb_st = _st0.enter_context(nc.sbuf_tensor("attnb_st", [128, 8], F32))
            ps0 = _st0.enter_context(nc.psum_tensor("ps0", [128, 512], F32))

            for dst, key, src in [(a_pos, 'a_pos', d_alog), (negbdt, 'negbdt', d_bdt),
                                  (convb_sb, 'convb', d_convb), (nextb_sb, 'nextb', d_nextb),
                                  (lng_sb, 'lng', d_lng), (attnb_st, 'attnb', d_attnb),
                                  (convw_sb, 'convw', d_convw), (dp_sb, 'dp', d_dp)]:
                E('sync', lambda s, dst=dst, src=src: s.dma_start(dst[:, :], src),
                  writes=[key], dma_sem='d_su_s')
            E('sync', lambda s: s.dma_start(b1_sb[:, 0:32], d_b1[0]), writes=['b1'], dma_sem='d_su_s')
            E('sync', lambda s: s.dma_start(b1_sb[:, 32:64], d_b1[1]), writes=['b1'], dma_sem='d_su_s')
            E('sync', lambda s: s.dma_start(b2_sb[:, 0:8], d_b2[0]), writes=['b2'], dma_sem='d_su_s')
            E('sync', lambda s: s.dma_start(b2_sb[:, 8:16], d_b2[1]), writes=['b2'], dma_sem='d_su_s')
            E('sync', lambda s: s.dma_start(stage[:, 80:88], d_lnb), writes=['stage_t'], dma_sem='d_su_s')

            E('gpsimd', lambda g: g.dma_start(d_winb, d_wint), writes=['cv_win'], dma_sem='d_cv_win')
            for g_i in range(16):
                E('gpsimd', lambda g, g_i=g_i: g.dma_start(
                    wxt_bf[:, g_i * 96:(g_i + 1) * 96], d_wxt[g_i * 128:(g_i + 1) * 128, :]),
                  writes=['wxt'], dma_sem='d_su_g')
            E('gpsimd', lambda g: g.dma_start(wdtt_bf[:, :], d_wdtt), writes=['wdtt'], dma_sem='d_su_g')

            E('scalar', lambda sc: sc.activation(a_pos[:, :], a_pos[:, :], AF.Exp),
              reads=['a_pos'], writes=['a_pos'])
            E('vector', lambda v: v.tensor_scalar_mul(a_pos[:, :], a_pos[:, :], -1.0),
              reads=['a_pos'], writes=['a_pos'])
            E('vector', lambda v: v.tensor_copy(lnb_bf[:, :], stage[:, 80:88]),
              reads=['stage_t'], writes=['lnb_bf'])

            for t, k in ((identf, 'identf'), (identb, 'identb')):
                E('gpsimd', lambda g, t=t: g.memset(t[:, :], 1.0), writes=[k])
                E('gpsimd', lambda g, t=t: g.affine_select(
                    t[:, :], t[:, :], [[1, 128]], OP.is_equal, 0.0, base=0,
                    channel_multiplier=-1), reads=[k], writes=[k])
            E('gpsimd', lambda g: g.memset(ones_bf[:, :], 1.0), writes=['ones_bf'])
            E('gpsimd', lambda g: g.memset(eps_sb[:, :], LN_EPS), writes=['eps'])
            E('gpsimd', lambda g: g.memset(one_sb[:, :], 1.0), writes=['one'])
            E('vector', lambda v: v.memset(sel[:, :], 1.0), writes=['sel'])
            E('gpsimd', lambda g: g.affine_select(
                sel[:, :].rearrange("p (t j) -> p t j", j=128),
                sel[:, :].rearrange("p (t j) -> p t j", j=128),
                [[1, 32], [0, 128]], OP.is_equal, 0.0, base=0, channel_multiplier=-1),
              reads=['sel'], writes=['sel'])
            for g_i in range(16):
                E('scalar', lambda sc, g_i=g_i: sc.activation(
                    diagdp[:, g_i * 128:(g_i + 1) * 128], ones_bf[:, :], AF.Copy,
                    scale=dp_sb[:, g_i:g_i + 1]),
                  reads=['ones_bf', 'dp'], writes=['diagdp'])
            E('gpsimd', lambda g: g.affine_select(
                diagdp[:, :].rearrange("p (t f) -> p t f", f=128),
                diagdp[:, :].rearrange("p (t f) -> p t f", f=128),
                [[0, 16], [1, 128]], OP.is_equal, 0.0, base=0, channel_multiplier=-1),
              reads=['diagdp'], writes=['diagdp'])

            for dc in range(8):
                E('sync', lambda s, dc=dc: s.dma_start(
                    stage[:, dc * D:(dc + 1) * D], d_attnw[dc * 128:(dc + 1) * 128, :]),
                  writes=['stage', 'stage_t'], dma_sem='d_su_s')
            E('vector', lambda v: v.tensor_copy(attn_st[:, :], stage[:, :]),
              reads=['stage'], writes=['attn_st'])
            for ec in range(8):
                for dc in range(8):
                    E('tensor', lambda te, ec=ec, dc=dc: te.matmul(
                        ps0[:, ec:ec + 1],
                        attn_st[:, dc * D + ec * 128: dc * D + (ec + 1) * 128],
                        lnb_bf[:, dc:dc + 1],
                        start=(dc == 0), stop=(dc == 7)),
                      reads=['lnb_bf', 'attn_st'], writes=['ps0'], inc=(dc == 7))
            E('scalar', lambda sc: sc.copy(biasp[:, :], ps0[:, 0:8]),
              reads=['ps0'], writes=['biasp'])
            E('vector', lambda v: v.tensor_add(biasp[:, :], biasp[:, :], attnb_st[:, :]),
              reads=['biasp', 'attnb'], writes=['biasp'])
            for dc in range(8):
                E('vector', lambda v, dc=dc: v.tensor_scalar(
                    attn_st[:, dc * D:(dc + 1) * D], stage[:, dc * D:(dc + 1) * D],
                    lng_sb[:, dc:dc + 1], None, OP.mult),
                  reads=['stage', 'lng'], writes=['attn_st'])
            E('sync', lambda s: s.dma_start(
                d_attnbf.rearrange("(dc p) e -> p dc e", p=128),
                attn_st[:, :].rearrange("p (dc e) -> p dc e", dc=8)),
              reads=['attn_st'], writes=['cv_at'], dma_sem='d_su_s')

            with nc.Block() as blk0:
                @blk0.sync
                def _(s):
                    em.run('sync', s, sems)

                @blk0.scalar
                def _(s):
                    em.run('scalar', s, sems)

                @blk0.vector
                def _(s):
                    em.run('vector', s, sems)

                @blk0.tensor
                def _(s):
                    em.run('tensor', s, sems)

                @blk0.gpsimd
                def _(s):
                    em.run('gpsimd', s, sems)
            em.ops = []
        em.fence_all()

        with contextlib.ExitStack() as _st1:
            _M = _st1.enter_context
            xn_fm = _M(nc.sbuf_tensor("xn_fm", [128, 2 * 8 * LC], F32))
            xn_fmb = _M(nc.sbuf_tensor("xn_fmb", [128, 8 * LC], BF16))
            ln2_fmb = _M(nc.sbuf_tensor("ln2_fmb", [128, 2 * 8 * LC], BF16))
            sz_bf = _M(nc.sbuf_tensor("sz_bf", [128, 2 * 16 * LC], BF16))
            gated = _M(nc.sbuf_tensor("gated", [128, 2 * 16 * LC], BF16))
            bc_bf = _M(nc.sbuf_tensor("bc_bf", [32, 2 * LC], BF16))
            dtlow_bf = _M(nc.sbuf_tensor("dtlow_bf", [DTR, LC], BF16))
            s_fm = _M(nc.sbuf_tensor("s_fm", [128, 8 * LC], F32))
            hcarry = _M(nc.sbuf_tensor("hcarry", [128, 256], F32))
            halo_sb = _M(nc.sbuf_tensor("halo_sb", [128, 16 * 3], F32))
            stats = _M(nc.sbuf_tensor("stats", [128, 32], F32))
            ball = _M(nc.sbuf_tensor("ball", [128, 16 * LC], BF16))
            call = _M(nc.sbuf_tensor("call", [128, 16 * LC], BF16))
            da0 = _M(nc.sbuf_tensor("da0", [128, NH * (LC + 1)], F32))
            da1 = _M(nc.sbuf_tensor("da1", [128, NH * (LC + 1)], F32))
            db0 = _M(nc.sbuf_tensor("db0", [128, NH * (LC + 1)], F32))
            db1 = _M(nc.sbuf_tensor("db1", [128, NH * (LC + 1)], F32))
            da_p = [da0, da1]
            dbx_p = [db0, db1]
            hc_p = [db0[:, :].bitcast(BF16)[:, 0:NH * LC],
                    db1[:, :].bitcast(BF16)[:, 0:NH * LC]]
            dtin = _M(nc.sbuf_tensor("dtin", [128, 3 * LC], F32))
            xcin = _M(nc.sbuf_tensor("xcin", [128, 3 * LC], F32))
            u_r = _M(nc.sbuf_tensor("u_r", [128, 2 * LC], F32))
            xcbf_r = _M(nc.sbuf_tensor("xcbf_r", [128, 2 * LC], BF16))
            xdbl_sb = _M(nc.sbuf_tensor("xdbl_sb", [96, LC], F32))
            pool = _M(nc.sbuf_tensor("pool", [128, 28672], BF16))

            def pv_f32(off_b, nelem):
                k = off_b // 2
                return pool[:, k:k + 2 * nelem].bitcast(F32)

            def pv_bf(off_b, nelem):
                k = off_b // 2
                return pool[:, k:k + nelem]

            x_tm = pv_f32(0, NTT * D)
            xn_tm = pv_f32(8192, NTT * D)
            ln2_tm = pv_f32(16384, NTT * D)
            trash = pv_f32(24576, D)
            winq = [pv_bf(0, 8 * 512), pv_bf(8192, 8 * 512)]
            xcfull = pv_f32(16384, 16 * (LC + 3))
            xcv_r = [pv_f32(33536, LC), pv_f32(34560, LC)]
            dtr_r = [pv_f32(35584, LC), pv_f32(36608, LC)]
            xcb_r = [pv_bf(37632, LC), pv_bf(38144, LC)]
            woutb = pv_bf(0, 16 * 1024)
            w1r = [pv_bf(0, 8 * 512), pv_bf(8192, 8 * 512)]
            w2r = [pv_bf(16384, 4 * 1024), pv_bf(24576, 4 * 1024)]
            h1r = [pv_bf(32768, 4 * LC), pv_bf(34816, 4 * LC)]
            xa_bf = pv_bf(36864, 8 * LC)
            y2_bf = pv_bf(40960, 8 * LC)
            gg = pv_f32(45056, 8 * LC)
            attn_r = [pv_bf(53248, 8 * 128), pv_bf(55296, 8 * 128)]
            next_r = attn_r
            xre = pv_f32(0, NTT * D)
            out_tm = pv_f32(8192, NTT * D)
            comb = pv_f32(16384, 8 * LC)

            POOL_REGION = {
                'x_tm': (0, 8192), 'xn_tm': (8192, 16384), 'ln2_tm': (16384, 24576),
                'trash': (24576, 28672),
                'winq0': (0, 8192), 'winq1': (8192, 16384), 'xcfull': (16384, 33152),
                'xcv0': (33536, 34560), 'xcv1': (34560, 35584),
                'dtr0': (35584, 36608), 'dtr1': (36608, 37632),
                'xcb0': (37632, 38144), 'xcb1': (38144, 38656),
                'woutb': (0, 32768),
                'w1b0': (0, 8192), 'w1b1': (8192, 16384),
                'w2b0': (16384, 24576), 'w2b1': (24576, 32768),
                'h10': (32768, 34816), 'h11': (34816, 36864),
                'xa': (36864, 40960), 'y2': (40960, 45056),
                'gg': (45056, 53248),
                'at0': (53248, 55296), 'at1': (55296, 57344),
                'xre': (0, 8192), 'out_tm': (8192, 16384), 'comb': (16384, 24576),
            }

            def povl(key):
                a0, a1 = POOL_REGION[key]
                return [k for k, (b0, b1) in POOL_REGION.items()
                        if k != key and b0 < a1 and a0 < b1]

            psA = _M(nc.psum_tensor("psA", [128, 512], F32))
            psB = _M(nc.psum_tensor("psB", [128, 512], F32))
            psY0 = _M(nc.psum_tensor("psY0", [128, 512], F32))
            psY1 = _M(nc.psum_tensor("psY1", [128, 512], F32))
            psY_p = [(psY0, 'psY0'), (psY1, 'psY1')]
            psY2 = _M(nc.psum_tensor("psY2", [128, 8 * LC], F32))
            GPS = [(psA, 'psA'), (psB, 'psB')]
            gps_i = [0]

            def P():
                ps, pn = GPS[gps_i[0] % 2]
                gps_i[0] += 1
                return ps, pn

            def phase_A(c):
                L = []
                par = c % 2
                t0 = c * LC
                xnp = xn_fm[:, par * 8 * LC:(par + 1) * 8 * LC]
                ln2p = ln2_fmb[:, par * 8 * LC:(par + 1) * 8 * LC]
                xnk, ln2k = f'xn{par}', f'ln2{par}'
                TH(L, 'sync', lambda s, t0=t0: s.dma_start(
                    x_tm[:, :].rearrange("p (tt d) -> p tt d", tt=NTT),
                    dx[t0:t0 + LC, :].rearrange("(tt p) d -> p tt d", p=128)),
                   writes=['x_tm'], dma_sem='d_xin', war=povl('x_tm'), est=7000)
                for src, dstk, dst in ((x_tm, 'xn_tm', xn_tm), (xn_tm, 'ln2_tm', ln2_tm)):
                    srck = 'x_tm' if src is x_tm else 'xn_tm'
                    for tt in range(NTT):
                        for hh in range(2):
                            TH(L, 'vector', lambda v, tt=tt, hh=hh, src=src: v.bn_stats(
                                stats[:, 12 * tt + 6 * hh:12 * tt + 6 * hh + 6],
                                src[:, tt * D + hh * 512: tt * D + (hh + 1) * 512]),
                               reads=[srck], writes=['stats'], est=680)
                        TH(L, 'vector', lambda v, tt=tt: v.bn_aggr(
                            stats[:, 24 + 2 * tt:26 + 2 * tt], stats[:, 12 * tt:12 * tt + 12]),
                           reads=['stats'], writes=['stats'], est=300)
                    TH(L, 'scalar', lambda sc: sc.activation(
                        stats[:, 25:28:2], stats[:, 25:28:2], AF.Ln,
                        bias=eps_sb[:, 0:1]),
                       reads=['stats', 'eps'], writes=['stats'])
                    TH(L, 'scalar', lambda sc: sc.activation(
                        stats[:, 25:28:2], stats[:, 25:28:2], AF.Exp,
                        scale=-0.5),
                       reads=['stats'], writes=['stats'])
                    for tt in range(NTT):
                        TH(L, 'vector', lambda v, tt=tt, src=src, dst=dst: v.tensor_scalar(
                            dst[:, tt * D:(tt + 1) * D], src[:, tt * D:(tt + 1) * D],
                            stats[:, 24 + 2 * tt:25 + 2 * tt], stats[:, 25 + 2 * tt:26 + 2 * tt],
                            OP.subtract, OP.mult),
                           reads=[srck, 'stats'], writes=[dstk],
                           war=(povl(dstk) if tt == 0 else ()))
                for src, srck, dst, dstk in ((xn_tm, 'xn_tm', xnp, xnk),
                                             (ln2_tm, 'ln2_tm', ln2p, ln2k)):
                    for tt in range(NTT):
                        for dc in range(8):
                            ps, pn = P()
                            TH(L, 'tensor', lambda te, tt=tt, dc=dc, ps=ps, src=src: te.transpose(
                                ps[:, 0:128], src[:, tt * D + dc * 128: tt * D + (dc + 1) * 128],
                                identf[:, :]),
                               reads=[srck, 'identf'], writes=[pn])
                            TH(L, 'scalar', lambda sc, tt=tt, dc=dc, ps=ps, dst=dst: sc.copy(
                                dst[:, dc * LC + tt * 128: dc * LC + tt * 128 + 128], ps[:, 0:128]),
                               reads=[pn], writes=[dstk])
                TH(L, 'vector', lambda v, xnp=xnp: v.tensor_copy(xn_fmb[:, :], xnp[:, :]),
                   reads=[xnk], writes=['xn_fmb'])
                return L

            def phase_B(c):
                L = []
                par = c % 2
                szp = sz_bf[:, par * 16 * LC:(par + 1) * 16 * LC]
                szk = f'sz{par}'
                bck = f'bc{par}'
                TH(L, 'vector', lambda v: v.tensor_copy(
                    xcfull[:, :].rearrange("p (g e) -> p g e", e=LC + 3)[:, :, 0:3],
                    halo_sb[:, :].rearrange("p (g e) -> p g e", e=3)),
                   reads=['halo'], writes=['xcfull'], war=povl('xcfull'))
                for q in range(8):
                    wq = winq[q % 2]
                    wqn = f'winq{q % 2}'
                    TH(L, 'gpsimd', lambda s, q=q, wq=wq: s.dma_start(
                        wq[:, :].rearrange("p (dc e) -> p dc e", dc=8),
                        d_winb.rearrange("(dc p) e -> p dc e", p=128)[:, :, q * 512:(q + 1) * 512]),
                       reads=['cv_win'], writes=[wqn], dma_sem=f'd_win{q % 2}',
                       war=(povl(wqn) if q < 2 else ()), est=9000)
                    for el in range(4):
                        ec = q * 4 + el
                        ps, pn = P()
                        for dc in range(8):
                            TH(L, 'tensor', lambda te, wq=wq, el=el, dc=dc, ps=ps: te.matmul(
                                ps[:, 0:LC],
                                wq[:, dc * 512 + el * 128: dc * 512 + (el + 1) * 128],
                                xn_fmb[:, dc * LC:(dc + 1) * LC],
                                start=(dc == 0), stop=(dc == 7)),
                               reads=[wqn, 'xn_fmb'], writes=[pn], inc=(dc == 7))
                        if ec < 16:
                            TH(L, 'scalar', lambda sc, ec=ec, ps=ps: sc.copy(
                                xcfull[:, ec * (LC + 3) + 3: ec * (LC + 3) + 3 + LC],
                                ps[:, 0:LC]),
                               reads=[pn], writes=['xcfull'])
                        else:
                            TH(L, 'scalar', lambda sc, ec=ec, ps=ps, szp=szp: sc.activation(
                                szp[:, (ec - 16) * LC:(ec - 15) * LC], ps[:, 0:LC], AF.Silu),
                               reads=[pn], writes=[szk], tab='s')
                xdblps = None
                for g_i in range(16):
                    base = g_i * (LC + 3)
                    xr = xcv_r[g_i % 2]
                    xrk = f'xcv{g_i % 2}'
                    xb = xcb_r[g_i % 2]
                    xbk = f'xcb{g_i % 2}'
                    TH(L, 'vector', lambda v, base=base, xr=xr, g_i=g_i: v.tensor_scalar(
                        xr[:, :], xcfull[:, base: base + LC],
                        convw_sb[:, g_i * 4: g_i * 4 + 1], None, OP.mult),
                       reads=['xcfull', 'convw'], writes=[xrk],
                       war=(povl(xrk) if g_i < 2 else ()))
                    for k in range(1, 4):
                        TH(L, 'vector', lambda v, base=base, xr=xr, g_i=g_i, k=k:
                           v.scalar_tensor_tensor(
                               xr[:, :], xcfull[:, base + k: base + k + LC],
                               convw_sb[:, g_i * 4 + k: g_i * 4 + k + 1],
                               xr[:, :], OP.mult, OP.add),
                           reads=['xcfull', 'convw', xrk], writes=[xrk])
                    TH(L, 'scalar', lambda sc, xr=xr, g_i=g_i: sc.activation(
                        xr[:, :], xr[:, :], AF.Silu, bias=convb_sb[:, g_i:g_i + 1]),
                       reads=[xrk, 'convb'], writes=[xrk], tab='s')
                    TH(L, 'scalar', lambda sc, xr=xr, xb=xb: sc.copy(xb[:, :], xr[:, :]),
                       reads=[xrk], writes=[xbk],
                       war=(povl(xbk) if g_i < 2 else ()))
                    TH(L, 'sync', lambda s, xr=xr, c=c, g_i=g_i: s.dma_start(
                        d_xcs[(c * 16 + g_i) * 128:(c * 16 + g_i + 1) * 128, :], xr[:, :]),
                       reads=[xrk], writes=[f'dxc{g_i}'], dma_sem=f'd_xco{g_i % 2}', est=3200)
                    psx, pxn = (xdblps if g_i % 4 != 0 else P())
                    if g_i % 4 == 0:
                        xdblps = (psx, pxn)
                    TH(L, 'tensor', lambda te, g_i=g_i, xb=xb, psx=psx: te.matmul(
                        psx[0:96, 0:LC], wxt_bf[:, g_i * 96:(g_i + 1) * 96],
                        xb[:, :], start=(g_i % 4 == 0), stop=(g_i % 4 == 3)),
                       reads=['wxt', xbk], writes=[pxn], inc=True)
                    if g_i % 4 == 3:
                        if g_i == 3:
                            TH(L, 'vector', lambda v, psx=psx: v.tensor_copy(
                                xdbl_sb[:, :], psx[0:96, 0:LC]),
                               reads=[pxn], writes=['xdbl'])
                        else:
                            TH(L, 'vector', lambda v, psx=psx: v.tensor_add(
                                xdbl_sb[:, :], psx[0:96, 0:LC], xdbl_sb[:, :]),
                               reads=[pxn, 'xdbl'], writes=['xdbl'])
                TH(L, 'vector', lambda v: v.tensor_copy(
                    halo_sb[:, :].rearrange("p (g e) -> p g e", e=3),
                    xcfull[:, :].rearrange("p (g e) -> p g e", e=LC + 3)[:, :, LC:LC + 3]),
                   reads=['xcfull'], writes=['halo'])
                TH(L, 'scalar', lambda sc, par=par: sc.copy(
                    bc_bf[:, par * LC:(par + 1) * LC], xdbl_sb[0:32, :]),
                   reads=['xdbl'], writes=[bck])
                TH(L, 'scalar', lambda sc: sc.copy(dtlow_bf[0:32, :], xdbl_sb[32:64, :]),
                   reads=['xdbl'], writes=['dtlow'])
                TH(L, 'scalar', lambda sc: sc.copy(dtlow_bf[32:64, :], xdbl_sb[64:96, :]),
                   reads=['xdbl'], writes=['dtlow'])
                for g_i in range(16):
                    ps, pn = P()
                    dr = dtr_r[g_i % 2]
                    drk = f'dtr{g_i % 2}'
                    TH(L, 'tensor', lambda te, g_i=g_i, ps=ps: te.matmul(
                        ps[:, 0:LC], wdtt_bf[:, g_i * 128:(g_i + 1) * 128],
                        dtlow_bf[:, :], start=True, stop=True),
                       reads=['wdtt', 'dtlow'], writes=[pn])
                    TH(L, 'scalar', lambda sc, g_i=g_i, ps=ps, dr=dr: sc.activation(
                        dr[:, :], ps[:, 0:LC], AF.Exp,
                        bias=negbdt[:, g_i:g_i + 1]),
                       reads=[pn, 'negbdt'], writes=[drk],
                       war=(povl(drk) if g_i < 2 else ()))
                    TH(L, 'scalar', lambda sc, dr=dr: sc.activation(
                        dr[:, :], dr[:, :], AF.Ln, bias=one_sb[:, 0:1]),
                       reads=[drk, 'one'], writes=[drk])
                    TH(L, 'sync', lambda s, dr=dr, c=c, g_i=g_i: s.dma_start(
                        d_dts[(c * 16 + g_i) * 128:(c * 16 + g_i + 1) * 128, :], dr[:, :]),
                       reads=[drk], writes=[f'ddt{g_i}'], dma_sem=f'd_dto{g_i % 2}', est=3200)
                return L

            def _c_slice_in(c, g_i):
                L = []
                r = g_i % 3
                TH(L, 'sync', lambda s, c=c, g_i=g_i, r=r: s.dma_start(
                    dtin[:, r * LC:(r + 1) * LC],
                    d_dts[(c * 16 + g_i) * 128:(c * 16 + g_i + 1) * 128, :]),
                   reads=[f'ddt{g_i}'], writes=[f'dti{r}'], dma_sem=f'd_dti{r}', est=3200)
                TH(L, 'sync', lambda s, c=c, g_i=g_i, r=r: s.dma_start(
                    xcin[:, r * LC:(r + 1) * LC],
                    d_xcs[(c * 16 + g_i) * 128:(c * 16 + g_i + 1) * 128, :]),
                   reads=[f'dxc{g_i}'], writes=[f'xci{r}'], dma_sem=f'd_xci{r}', est=3200)
                return L

            def phase_C_prologue(c):
                L = []
                par = c % 2
                bck = f'bc{par}'
                bcs = bc_bf[:, par * LC:(par + 1) * LC]
                for n in range(16):
                    ps, pn = P()
                    TH(L, 'tensor', lambda te, n=n, ps=ps, bcs=bcs: te.matmul(
                        ps[:, 0:LC], sel[:, n * 128:(n + 1) * 128], bcs,
                        start=True, stop=True),
                       reads=['sel', bck], writes=[pn])
                    TH(L, 'scalar', lambda sc, n=n, ps=ps: sc.copy(
                        ball[:, n * LC:(n + 1) * LC], ps[:, 0:LC]),
                       reads=[pn], writes=['ball'])
                    ps, pn = P()
                    TH(L, 'tensor', lambda te, n=n, ps=ps, bcs=bcs: te.matmul(
                        ps[:, 0:LC], sel[:, (16 + n) * 128:(17 + n) * 128], bcs,
                        start=True, stop=True),
                       reads=['sel', bck], writes=[pn])
                    TH(L, 'scalar', lambda sc, n=n, ps=ps: sc.copy(
                        call[:, n * LC:(n + 1) * LC], ps[:, 0:LC]),
                       reads=[pn], writes=['call'])
                for g_i in range(3):
                    L += _c_slice_in(c, g_i)
                return L

            def _c_gate(c, g_i):
                L = []
                par = c % 2
                szk = f'sz{par}'
                gtk = f'gated{par}'
                gtp = gated[:, par * 16 * LC:(par + 1) * 16 * LC]
                psy, psyk = psY_p[g_i % 2]
                TH(L, 'vector', lambda v, g_i=g_i, gtp=gtp, par=par, psy=psy: v.tensor_mul(
                    gtp[:, g_i * LC:(g_i + 1) * LC], psy[:, 0:LC],
                    sz_bf[:, (par * 16 + g_i) * LC:(par * 16 + g_i + 1) * LC]),
                   reads=[psyk, szk], writes=[gtk])
                return L

            def phase_C_iter(c, g_i):
                L = []
                par = c % 2
                r = g_i % 3
                r2 = g_i % 2
                szk = f'sz{par}'
                gtk = f'gated{par}'
                gtp = gated[:, par * 16 * LC:(par + 1) * 16 * LC]
                dts = dtin[:, r * LC:(r + 1) * LC]
                xcs = xcin[:, r * LC:(r + 1) * LC]
                us = u_r[:, r2 * LC:(r2 + 1) * LC]
                xbs = xcbf_r[:, r2 * LC:(r2 + 1) * LC]
                TH(L, 'vector', lambda v, dts=dts, xcs=xcs, us=us: v.tensor_mul(
                    us, dts, xcs),
                   reads=[f'dti{r}', f'xci{r}'], writes=[f'u{r2}'])
                TH(L, 'scalar', lambda sc, xcs=xcs, xbs=xbs: sc.copy(xbs, xcs),
                   reads=[f'xci{r}'], writes=[f'xcb2{r2}'])
                for half in range(2):
                    p_i = (g_i * 2 + half) % 2
                    da = da_p[p_i]
                    dbx = dbx_p[p_i]
                    hc = hc_p[p_i]
                    dak = f'da{p_i}'
                    dbk = f'dbx{p_i}'
                    TH(L, 'gpsimd', lambda g, g_i=g_i, half=half, dbx=dbx: g.tensor_copy(
                        dbx[:, 0::LC + 1],
                        hcarry[:, g_i * 16 + half * NH: g_i * 16 + half * NH + NH]),
                       reads=['hcarry'], writes=[dbk], est=700)
                    for i in range(NH):
                        n = half * NH + i
                        TH(L, 'scalar', lambda sc, g_i=g_i, i=i, n=n, da=da, dts=dts: sc.activation(
                            da[:, i * (LC + 1) + 1:(i + 1) * (LC + 1)], dts, AF.Exp,
                            scale=a_pos[:, g_i * 16 + n: g_i * 16 + n + 1]),
                           reads=[f'dti{r}', 'a_pos'], writes=[dak], est=650)
                    for i in range(NH):
                        n = half * NH + i
                        TH(L, 'vector', lambda v, i=i, n=n, dbx=dbx, us=us: v.tensor_tensor(
                            dbx[:, i * (LC + 1) + 1:(i + 1) * (LC + 1)], us,
                            ball[:, n * LC:(n + 1) * LC], OP.mult),
                           reads=[f'u{r2}', 'ball'], writes=[dbk], est=450)
                    TH(L, 'vector', lambda v, da=da, dbx=dbx: v.tensor_tensor_scan(
                        dbx[:, 0:NH * (LC + 1)],
                        da[:, 0:NH * (LC + 1)],
                        dbx[:, 0:NH * (LC + 1)],
                        0.0, OP.mult, OP.add),
                       reads=[dak, dbk], writes=[dbk], est=4600)
                    TH(L, 'gpsimd', lambda g, g_i=g_i, half=half, dbx=dbx: g.tensor_copy(
                        hcarry[:, g_i * 16 + half * NH: g_i * 16 + half * NH + NH],
                        dbx[:, LC::LC + 1]),
                       reads=[dbk], writes=['hcarry'], est=700)
                    TH(L, 'gpsimd', lambda g, half=half, dbx=dbx, hc=hc: g.tensor_tensor(
                        hc[:, :].rearrange("p (n l) -> p n l", n=NH),
                        dbx[:, :].rearrange("p (n l) -> p n l", l=LC + 1)[:, :, 1:LC + 1],
                        call[:, half * NH * LC:(half + 1) * NH * LC]
                        .rearrange("p (n l) -> p n l", n=NH), OP.mult),
                       reads=[dbk, 'call'], writes=[dbk], est=4300)
                    psy, psyk = psY_p[g_i % 2]
                    for i in range(NH):
                        TH(L, 'tensor', lambda te, i=i, half=half, hc=hc, psy=psy: te.matmul(
                            psy[:, 0:LC], identb[:, :], hc[:, i * LC:(i + 1) * LC],
                            start=(half == 0 and i == 0), stop=False),
                           reads=[dbk, 'identb'], writes=[psyk], inc=(i == NH - 1))
                psy, psyk = psY_p[g_i % 2]
                TH(L, 'tensor', lambda te, g_i=g_i, xbs=xbs, psy=psy: te.matmul(
                    psy[:, 0:LC], diagdp[:, g_i * 128:(g_i + 1) * 128],
                    xbs, start=False, stop=True),
                   reads=['diagdp', f'xcb2{r2}'], writes=[psyk])
                if g_i >= 1:
                    L += _c_gate(c, g_i - 1)
                if g_i == 15:
                    L += _c_gate(c, 15)
                if g_i + 3 < 16:
                    L += _c_slice_in(c, g_i + 3)
                return L

            def phase_D(c):
                L = []
                par = c % 2
                gtk = f'gated{par}'
                gtp = gated[:, par * 16 * LC:(par + 1) * 16 * LC]
                TH(L, 'gpsimd', lambda s: s.dma_start(
                    woutb[:, :].rearrange("p (gi e) -> p gi e", gi=16),
                    d_woutb.rearrange("(gi p) e -> p gi e", p=128)),
                   reads=['cv_wout'], writes=['woutb'], dma_sem='d_wout', war=povl('woutb'), est=26000)
                for m in range(8):
                    ps, pn = P()
                    for g_i in range(16):
                        TH(L, 'tensor', lambda te, m=m, g_i=g_i, ps=ps, gtp=gtp: te.matmul(
                            ps[:, 0:LC],
                            woutb[:, g_i * 1024 + m * 128: g_i * 1024 + (m + 1) * 128],
                            gtp[:, g_i * LC:(g_i + 1) * LC],
                            start=(g_i == 0), stop=(g_i == 15)),
                           reads=['woutb', gtk], writes=[pn], inc=(g_i == 15))
                    TH(L, 'scalar', lambda sc, m=m, ps=ps: sc.copy(
                        s_fm[:, m * LC:(m + 1) * LC], ps[:, 0:LC]),
                       reads=[pn], writes=['s_fm'])
                return L

            def phase_E(c):
                L = []
                par = c % 2
                exp_i = 0 if c * LC < SEQ // 2 else 1
                xnk, ln2k = f'xn{par}', f'ln2{par}'
                xnp = xn_fm[:, par * 8 * LC:(par + 1) * 8 * LC]
                ln2p = ln2_fmb[:, par * 8 * LC:(par + 1) * 8 * LC]

                def w1_load(fb):
                    TH(L, 'gpsimd', lambda s, fb=fb, exp_i=exp_i: s.dma_start(
                        w1r[fb % 2][:, :].rearrange("p (dc f) -> p dc f", dc=8),
                        d_w1b[exp_i].rearrange("(dc p) f -> p dc f", p=128)
                        [:, :, fb * 512:(fb + 1) * 512]),
                       reads=['cv_w1'], writes=[f'w1b{fb % 2}'], dma_sem=f'd_w1{fb % 2}',
                       war=(povl(f'w1b{fb % 2}') if fb < 2 else ()), est=9000)

                def w2_load(fb):
                    TH(L, 'gpsimd', lambda s, fb=fb, exp_i=exp_i: s.dma_start(
                        w2r[fb % 2][:, :].rearrange("p (fp e) -> p fp e", fp=4),
                        d_w2b[exp_i].rearrange("(fq p) e -> p fq e", p=128)
                        [:, fb * 4:(fb + 1) * 4, :]),
                       reads=['cv_w2'], writes=[f'w2b{fb % 2}'], dma_sem=f'd_w2{fb % 2}',
                       war=(povl(f'w2b{fb % 2}') if fb < 2 else ()), est=9000)

                for m in range(8):
                    ar = attn_r[m % 2]
                    ark = f'at{m % 2}'
                    TH(L, 'gpsimd', lambda s, m=m, ar=ar: s.dma_start(
                        ar[:, :].rearrange("p (dc e) -> p dc e", dc=8),
                        d_attnbf.rearrange("(dc p) e -> p dc e", p=128)
                        [:, :, m * 128:(m + 1) * 128]),
                       reads=['cv_at'], writes=[ark], dma_sem=f'd_at{m % 2}',
                       war=(povl(ark) if m < 2 else ()))
                    ps, pn = P()
                    for dc in range(8):
                        TH(L, 'tensor', lambda te, m=m, dc=dc, ps=ps, ar=ar, ln2p=ln2p: te.matmul(
                            ps[:, 0:LC], ar[:, dc * 128:(dc + 1) * 128],
                            ln2p[:, dc * LC:(dc + 1) * LC],
                            start=(dc == 0), stop=(dc == 7)),
                           reads=[ark, ln2k], writes=[pn], inc=(dc == 7))
                    TH(L, 'vector', lambda v, m=m, ps=ps, xnp=xnp: v.scalar_tensor_tensor(
                        xa_bf[:, m * LC:(m + 1) * LC], ps[:, 0:LC],
                        biasp[:, m:m + 1], xnp[:, m * LC:(m + 1) * LC],
                        OP.add, OP.add),
                       reads=[pn, 'biasp', xnk], writes=['xa'],
                       war=(povl('xa') if m == 0 else ()))
                w1_load(0)
                w1_load(1)
                w2_load(0)
                for fb in range(8):
                    w1b = w1r[fb % 2]
                    w1k = f'w1b{fb % 2}'
                    w2b = w2r[fb % 2]
                    w2k = f'w2b{fb % 2}'
                    h1b = h1r[fb % 2]
                    h1k = f'h1{fb % 2}'
                    if fb + 1 < 8:
                        w2_load(fb + 1)
                    for fc in range(4):
                        ps, pn = P()
                        for dc in range(8):
                            TH(L, 'tensor', lambda te, fc=fc, dc=dc, ps=ps, w1b=w1b: te.matmul(
                                ps[:, 0:LC],
                                w1b[:, dc * 512 + fc * 128: dc * 512 + (fc + 1) * 128],
                                xa_bf[:, dc * LC:(dc + 1) * LC],
                                start=(dc == 0), stop=(dc == 7)),
                               reads=[w1k, 'xa'], writes=[pn], inc=(dc == 7))
                        TH(L, 'scalar', lambda sc, fb=fb, fc=fc, ps=ps, exp_i=exp_i, h1b=h1b: sc.activation(
                            h1b[:, fc * LC:(fc + 1) * LC], ps[:, 0:LC], AF.Gelu,
                            bias=b1_sb[:, exp_i * 32 + fb * 4 + fc: exp_i * 32 + fb * 4 + fc + 1]),
                           reads=[pn, 'b1'], writes=[h1k],
                           war=(povl(h1k) if fb < 2 and fc == 0 else ()), tab='g')
                    if fb + 2 < 8:
                        w1_load(fb + 2)
                    for m in range(8):
                        for fp in range(4):
                            TH(L, 'tensor', lambda te, fb=fb, m=m, fp=fp, w2b=w2b, h1b=h1b: te.matmul(
                                psY2[:, m * LC:(m + 1) * LC],
                                w2b[:, fp * 1024 + m * 128: fp * 1024 + (m + 1) * 128],
                                h1b[:, fp * LC:(fp + 1) * LC],
                                start=(fb == 0 and fp == 0 and m % 2 == 0),
                                stop=(fb == 7 and fp == 3 and m % 2 == 1)),
                               reads=[w2k, h1k], writes=['psY2'],
                               inc=(m == 7 and fp == 3))
                for m in range(8):
                    TH(L, 'scalar', lambda sc, m=m, exp_i=exp_i: sc.activation(
                        y2_bf[:, m * LC:(m + 1) * LC], psY2[:, m * LC:(m + 1) * LC],
                        AF.Identity, bias=b2_sb[:, exp_i * 8 + m: exp_i * 8 + m + 1]),
                       reads=['psY2', 'b2'], writes=['y2'],
                       war=(povl('y2') if m == 0 else ()))
                for m in range(8):
                    nr = next_r[m % 2]
                    nrk = f'at{m % 2}'
                    TH(L, 'gpsimd', lambda s, m=m, nr=nr: s.dma_start(
                        nr[:, :].rearrange("p (dc e) -> p dc e", dc=8),
                        d_nxwbf.rearrange("(dc p) e -> p dc e", p=128)
                        [:, :, m * 128:(m + 1) * 128]),
                       reads=['cv_nx'], writes=[nrk], dma_sem=f'd_nx{m % 2}',
                       war=(povl(nrk) if m < 2 else ()))
                    ps, pn = P()
                    for dc in range(8):
                        TH(L, 'tensor', lambda te, m=m, dc=dc, ps=ps, nr=nr: te.matmul(
                            ps[:, 0:LC], nr[:, dc * 128:(dc + 1) * 128],
                            y2_bf[:, dc * LC:(dc + 1) * LC],
                            start=(dc == 0), stop=(dc == 7)),
                           reads=[nrk, 'y2'], writes=[pn], inc=(dc == 7))
                    TH(L, 'scalar', lambda sc, m=m, ps=ps: sc.activation(
                        gg[:, m * LC:(m + 1) * LC], ps[:, 0:LC], AF.Gelu,
                        bias=nextb_sb[:, m:m + 1]),
                       reads=[pn, 'nextb'], writes=['gg'],
                       war=(povl('gg') if m == 0 else ()), tab='g')
                TH(L, 'scalar', lambda sc: sc.activation(gg[:, :], gg[:, :], AF.Silu),
                   reads=['gg'], writes=['gg'], tab='s')
                return L

            def phase_F(c):
                L = []
                t0 = c * LC
                TH(L, 'sync', lambda s, t0=t0: s.dma_start(
                    xre[:, :].rearrange("p (tt d) -> p tt d", tt=NTT),
                    dx[t0:t0 + LC, :].rearrange("(tt p) d -> p tt d", p=128)),
                   writes=['xre'], dma_sem='d_xre', war=povl('xre'), est=7000)
                TH(L, 'vector', lambda v: v.tensor_mul(comb[:, :], s_fm[:, :], gg[:, :]),
                   reads=['s_fm', 'gg'], writes=['comb'], war=povl('comb'), est=2400)
                for i, (m, tt) in enumerate([(m, tt) for tt in range(NTT) for m in range(8)]):
                    ps, pn = P()
                    TH(L, 'tensor', lambda te, m=m, tt=tt, ps=ps: te.transpose(
                        ps[:, 0:128], comb[:, m * LC + tt * 128: m * LC + tt * 128 + 128],
                        identf[:, :]),
                       reads=['comb', 'identf'], writes=[pn])
                    TH(L, 'vector', lambda v, m=m, tt=tt, ps=ps: v.tensor_add(
                        out_tm[:, tt * D + m * 128: tt * D + (m + 1) * 128],
                        ps[:, 0:128],
                        xre[:, tt * D + m * 128: tt * D + (m + 1) * 128]),
                       reads=[pn, 'xre'], writes=['out_tm'],
                       war=(povl('out_tm') if i == 0 else ()))
                TH(L, 'sync', lambda s, t0=t0: s.dma_start(
                    d_out[t0:t0 + LC, :].rearrange("(tt p) d -> p tt d", p=128),
                    out_tm[:, :].rearrange("p (tt d) -> p tt d", tt=NTT)),
                   reads=['out_tm'], dma_sem='d_xout', est=7000)
                return L

            def emit_list(L):
                for eng, build, kw in L:
                    E(eng, build, **kw)

            prog = []
            TH(prog, 'gpsimd', lambda g: g.dma_start(d_woutb, d_woutt),
               writes=['cv_wout'], dma_sem='d_cv_wout', est=12000)
            for i in range(2):
                TH(prog, 'gpsimd', lambda g, i=i: g.dma_start(d_w1b[i], d_w1[i]),
                   writes=['cv_w1'], dma_sem='d_cv_w1', est=30000)
                TH(prog, 'gpsimd', lambda g, i=i: g.dma_start(d_w2b[i], d_w2[i]),
                   writes=['cv_w2'], dma_sem='d_cv_w2', est=30000)
            TH(prog, 'gpsimd', lambda g: g.dma_start(d_nxwbf, d_nextw),
               writes=['cv_nx'], dma_sem='d_cv_nx', est=12000)
            TH(prog, 'vector', lambda v: v.memset(hcarry[:, :], 0.0), writes=['hcarry'])
            TH(prog, 'vector', lambda v: v.memset(halo_sb[:, :], 0.0), writes=['halo'])
            TH(prog, 'scalar', lambda sc: sc.memzero(da0[:, 0::LC + 1]), writes=['da0'])
            TH(prog, 'scalar', lambda sc: sc.memzero(da1[:, 0::LC + 1]), writes=['da1'])
            prog += phase_A(0)
            prog += phase_B(0)

            def stamp(L, t):
                for eng, build, kw in L:
                    kw['slot'] = t
                return L

            for s in range(NCHUNK + 1):
                bg = []
                if s >= 1:
                    bg += phase_D(s - 1) + phase_E(s - 1) + phase_F(s - 1)
                if s + 1 < NCHUNK:
                    bg += phase_A(s + 1) + phase_B(s + 1)
                stamp(bg, s)
                if s < NCHUNK:
                    prog += stamp(phase_C_prologue(s), s)
                    nb = max(1, (len(bg) + 15) // 16)
                    for g_i in range(16):
                        prog += stamp(phase_C_iter(s, g_i), s)
                        prog += bg[g_i * nb:(g_i + 1) * nb]
                    prog += bg[16 * nb:]
                else:
                    prog += bg
            emit_list(schedule_ops(prog))

            final_waits = [(dn, em.cnt[dn]) for dn in DMA_SEMS if em.cnt[dn] > 0]
            em.ops.append(('sync', lambda s: None, final_waits, None, None))

            with nc.Block() as blk1:
                @blk1.sync
                def _(s):
                    em.run('sync', s, sems)

                @blk1.scalar
                def _(s):
                    em.run('scalar', s, sems)

                @blk1.vector
                def _(s):
                    em.run('vector', s, sems)

                @blk1.tensor
                def _(s):
                    em.run('tensor', s, sems)

                @blk1.gpsimd
                def _(s):
                    em.run('gpsimd', s, sems)

    return nc


def _layout_inputs(inputs):
    A = np.ascontiguousarray
    x = inputs['L0_attn_x']
    W_inT = A(inputs['L0_attn_W_in'].T)
    perm = list(range(64, 80)) + list(range(80, 96)) + list(range(0, 64))
    W_xTp = A(inputs['L0_attn_W_x'][perm].T)
    W_dtT = A(inputs['L0_attn_W_dt'].T)
    convw = A(inputs['L0_attn_conv_w'].reshape(16, 128, 4).transpose(1, 0, 2).reshape(128, 64))
    convb = A(inputs['L0_attn_conv_b'].reshape(16, 128).T)
    b_dt = A(inputs['L0_attn_b_dt'].reshape(16, 128).T)
    A_log = A(inputs['L0_attn_A_log'].reshape(16, 128, 16).transpose(1, 0, 2).reshape(128, 256))
    Dp = A(inputs['L0_attn_Dp'].reshape(16, 128).T)
    W_outT = A(inputs['L0_attn_W_out'].T)
    ln_g = A(inputs['L0_ffn_ln_g'].reshape(8, 128).T)
    ln_b = A(inputs['L0_ffn_ln_b'].reshape(8, 128).T)
    attn_w = A(inputs['L0_ffn_attn_w'])
    attn_b = A(inputs['L0_ffn_attn_b'].reshape(8, 128).T)
    w1 = inputs['L0_ffn_w1']
    b1 = inputs['L0_ffn_b1']
    w2 = inputs['L0_ffn_w2']
    b2 = inputs['L0_ffn_b2']
    next_w = A(inputs['L0_ffn_next_w'])
    next_b = A(inputs['L0_ffn_next_b'].reshape(8, 128).T)

    in_maps = []
    for b in range(B_FULL):
        e0 = 2 * (b % 4)
        e1 = e0 + 1
        in_maps.append({
            "x": A(x[b]),
            "W_inT": W_inT,
            "convw": convw,
            "convb": convb,
            "W_xTp": W_xTp,
            "W_dtT": W_dtT,
            "b_dt": b_dt,
            "A_log": A_log,
            "Dp": Dp,
            "W_outT": W_outT,
            "ln_g": ln_g,
            "ln_b": ln_b,
            "attn_w": attn_w,
            "attn_b": attn_b,
            "w1A": A(w1[e0]),
            "w1B": A(w1[e1]),
            "b1A": A(b1[e0].reshape(32, 128).T),
            "b1B": A(b1[e1].reshape(32, 128).T),
            "w2A": A(w2[e0]),
            "w2B": A(w2[e1]),
            "b2A": A(b2[e0].reshape(8, 128).T),
            "b2B": A(b2[e1].reshape(8, 128).T),
            "next_w": next_w,
            "next_b": next_b,
        })
    return in_maps


def kernel(**inputs):
    inputs = {k: np.asarray(v, dtype=np.float32) for k, v in inputs.items()}
    in_maps = _layout_inputs(inputs)
    nc = build_nc()
    res = run_bass_kernel_spmd(nc, in_maps, core_ids=list(range(B_FULL)))
    out = np.stack([res.results[b]["out"] for b in range(B_FULL)], axis=0)
    return out


# version 31
# speedup vs baseline: 1.0284x; 1.0132x over previous
import contextlib
import numpy as np
import concourse.bass as bass
import concourse.mybir as mybir
from concourse.alu_op_type import AluOpType
from concourse.bass_utils import run_bass_kernel_spmd

F32 = mybir.dt.float32
BF16 = mybir.dt.bfloat16
AF = mybir.ActivationFunctionType
OP = AluOpType

B_FULL = 8
SEQ = 4096
D = 1024
DI = 2048
E2 = 4096
DFF = 4096
NST = 16
DTR = 64
LC = 256
NCHUNK = SEQ // LC
NTT = LC // 128
NH = 8
LN_EPS = 1e-5

DMA_SEMS = ('d_su_s', 'd_su_g', 'd_cv_win', 'd_cv_wout', 'd_cv_w1', 'd_cv_w2', 'd_cv_nx', 'd_xin', 'd_xre', 'd_xout',
            'd_win0', 'd_win1', 'd_wout', 'd_w10', 'd_w11', 'd_w20', 'd_w21',
            'd_at0', 'd_at1', 'd_nx0', 'd_nx1',
            'd_dto0', 'd_dto1', 'd_xco0', 'd_xco1',
            'd_dti0', 'd_dti1', 'd_dti2', 'd_dti3', 'd_xci0', 'd_xci1', 'd_xci2', 'd_xci3')


class Emitter:

    def __init__(self):
        self.ops = []
        self.cnt = {}
        self.writer = {}
        self.readers = {}
        self.pending_reads = {}
        self.pending_writes = {}
        self.watermark = {}
        for e in ('sync', 'scalar', 'vector', 'tensor', 'gpsimd'):
            self.cnt['e_' + e] = 0
            self.pending_reads[e] = []
            self.pending_writes[e] = []
        for dn in DMA_SEMS:
            self.cnt[dn] = 0

    def _deps(self, reads, writes):
        deps = {}
        for b in reads:
            t = self.writer.get(b)
            if t:
                deps[t[0]] = max(deps.get(t[0], 0), t[1])
        for b in writes:
            t = self.writer.get(b)
            if t:
                deps[t[0]] = max(deps.get(t[0], 0), t[1])
            for t in self.readers.get(b, ()):
                deps[t[0]] = max(deps.get(t[0], 0), t[1])
        return deps

    def emit(self, eng, build, reads=(), writes=(), inc=True, dma_sem=None, war=(), est=None, tab=None, slot=None):
        reads = list(reads)
        writes = list(writes)
        deps = self._deps(reads, list(writes) + list(war))
        waits = []
        for s, v in deps.items():
            if s.startswith('d_'):
                v = self.cnt[s]
            if self.watermark.get((eng, s), -1) < v:
                waits.append((s, v))
                self.watermark[(eng, s)] = v
        if dma_sem is not None:
            self.cnt[dma_sem] += 16
            token = (dma_sem, self.cnt[dma_sem])
        elif inc:
            self.cnt['e_' + eng] += 1
            token = ('e_' + eng, self.cnt['e_' + eng])
        else:
            token = None

        if token is not None:
            reads += self.pending_reads[eng]
            writes_all = writes + self.pending_writes[eng]
            self.pending_reads[eng] = []
            self.pending_writes[eng] = []
            for b in reads:
                self.readers.setdefault(b, set()).add(token)
            for b in writes_all:
                self.writer[b] = token
                self.readers[b] = set()
        else:
            self.pending_reads[eng] += reads
            self.pending_writes[eng] += writes

        self.ops.append((eng, build, waits, token, dma_sem))
        return token

    def fence_all(self):
        waits = [(k, v) for k, v in self.cnt.items() if v > 0]
        for e in ('sync', 'scalar', 'vector', 'tensor', 'gpsimd'):
            mywaits = []
            for s, v in waits:
                if self.watermark.get((e, s), -1) < v:
                    mywaits.append((s, v))
                    self.watermark[(e, s)] = v
            self.ops.append((e, lambda s: None, mywaits, None, None))

    def run(self, eng_name, section, sems):
        for e, build, waits, token, dma_sem in self.ops:
            if e != eng_name:
                continue
            for s, v in waits:
                section.wait_ge(sems[s], v)
            ins = build(section)
            if token is not None and ins is not None:
                if dma_sem is not None:
                    ins.then_inc(sems[dma_sem], 16)
                else:
                    ins.then_inc(sems[token[0]], 1)


def TH(lst, eng, build, **kw):
    lst.append((eng, build, kw))


ENGS = ('sync', 'scalar', 'vector', 'tensor', 'gpsimd')
DEF_EST = {'sync': 150, 'scalar': 520, 'vector': 520, 'tensor': 190, 'gpsimd': 1100}


def schedule_ops(prog):
    n = len(prog)
    deps = [set() for _ in range(n)]
    writer = {}
    readers = {}
    pend_r = {e: [] for e in ENGS}
    pend_w = {e: [] for e in ENGS}
    pend_idx = {e: [] for e in ENGS}
    for idx, (eng, build, kw) in enumerate(prog):
        reads = list(kw.get('reads', ()))
        writes = list(kw.get('writes', ()))
        war = list(kw.get('war', ()))
        dma = kw.get('dma_sem')
        inc = kw.get('inc', True)
        for b in reads:
            w = writer.get(b)
            if w is not None:
                deps[idx].add(w)
        for b in writes + war:
            w = writer.get(b)
            if w is not None:
                deps[idx].add(w)
            for rd in readers.get(b, ()):
                deps[idx].add(rd)
        if dma is not None or inc:
            allr = reads + pend_r[eng]
            allw = writes + pend_w[eng]
            for j in pend_idx[eng]:
                deps[idx].add(j)
            pend_r[eng] = []
            pend_w[eng] = []
            pend_idx[eng] = []
            for b in allr:
                readers.setdefault(b, set()).add(idx)
            for b in allw:
                writer[b] = idx
                readers[b] = set()
        else:
            pend_r[eng] += reads
            pend_w[eng] += writes
            pend_idx[eng].append(idx)
        deps[idx].discard(idx)

    unit_of = [None] * n
    units = []
    openu = {e: None for e in ENGS}
    for idx, (eng, build, kw) in enumerate(prog):
        inc = kw.get('inc', True)
        dma = kw.get('dma_sem')
        if openu[eng] is not None:
            u = openu[eng]
        else:
            u = len(units)
            units.append([])
            openu[eng] = u
        units[u].append(idx)
        unit_of[idx] = u
        if inc or dma is not None:
            openu[eng] = None
    nu = len(units)
    udeps = [set() for _ in range(nu)]
    for idx in range(n):
        u = unit_of[idx]
        for d in deps[idx]:
            du = unit_of[d]
            if du != u:
                udeps[u].add(du)
    ueng = [prog[units[u][0]][0] for u in range(nu)]
    utab = []
    for u in range(nu):
        t = None
        if ueng[u] == 'scalar':
            t = 'e'
            for idx in units[u]:
                t2 = prog[idx][2].get('tab')
                if t2:
                    t = t2
        utab.append(t)
    udur = []
    uisdma = []
    for u in range(nu):
        d = 0
        isdma = False
        for idx in units[u]:
            kw = prog[idx][2]
            if kw.get('dma_sem') is not None:
                isdma = True
                d += kw.get('est', 5000)
            else:
                d += kw.get('est', DEF_EST[prog[idx][0]])
        udur.append(d)
        uisdma.append(isdma)
    succ = [[] for _ in range(nu)]
    for u in range(nu):
        for d in udeps[u]:
            succ[d].append(u)
    height = [0] * nu
    for u in range(nu - 1, -1, -1):
        h = 0
        for s2 in succ[u]:
            if height[s2] > h:
                h = height[s2]
        height[u] = h + udur[u]
    maxslot = 0
    uslot = []
    for u in range(nu):
        t = None
        for idx in units[u]:
            t2 = prog[idx][2].get('slot')
            if t2 is not None:
                t = t2 if t is None else min(t, t2)
        uslot.append(t)
        if t is not None and t > maxslot:
            maxslot = t
    WAGE = 25000
    for u in range(nu):
        if uslot[u] is not None and ueng[u] != 'vector':
            height[u] += WAGE * (maxslot - uslot[u])
    ndeps = [len(udeps[u]) for u in range(nu)]
    done_t = [0.0] * nu
    ready_t = [0.0] * nu
    avail = {e: 0.0 for e in ENGS}
    remaining = {e: 0 for e in ENGS}
    for u in range(nu):
        remaining[ueng[u]] += 1
    ready = {e: [] for e in ENGS}
    for u in range(nu):
        if ndeps[u] == 0:
            ready[ueng[u]].append(u)
    order = []
    scheduled = [False] * nu
    cur_tab = ['e']
    INF = float('inf')
    while len(order) < nu:
        e = None
        for e2 in ENGS:
            if remaining[e2] and ready[e2]:
                if e is None or avail[e2] < avail[e]:
                    e = e2
        if e is None:
            best = None
            for e2 in ENGS:
                for u in ready[e2]:
                    if best is None or ready_t[u] < ready_t[best]:
                        best = u
            assert best is None
            break
        now = avail[e]
        cands = [u for u in ready[e] if ready_t[u] <= now]
        if cands:
            if e == 'scalar':
                u = max(cands, key=lambda x: height[x]
                        + (2600 if utab[x] == cur_tab[0] else 0))
            else:
                u = max(cands, key=lambda x: height[x])
        else:
            u = min(ready[e], key=lambda x: (ready_t[x], -height[x]))
        ready[e].remove(u)
        scheduled[u] = True
        remaining[e] -= 1
        start = max(avail[e], ready_t[u])
        dur = udur[u]
        if e == 'scalar' and utab[u] is not None:
            if utab[u] != cur_tab[0]:
                dur += 1300
            cur_tab[0] = utab[u]
        if uisdma[u]:
            avail[e] = start + 200
        else:
            avail[e] = start + dur
        done_t[u] = start + dur
        order.append(u)
        for s2 in succ[u]:
            ndeps[s2] -= 1
            if done_t[u] > ready_t[s2]:
                ready_t[s2] = done_t[u]
            if ndeps[s2] == 0:
                ready[ueng[s2]].append(s2)
    assert len(order) == nu, (len(order), nu)
    out = []
    for u in order:
        for idx in units[u]:
            out.append(prog[idx])
    return out



def build_nc():
    nc = bass.Bass('TRN2', target_bir_lowering=False)

    dx = nc.dram_tensor("x", [SEQ, D], F32, kind="ExternalInput").ap()
    d_wint = nc.dram_tensor("W_inT", [D, E2], F32, kind="ExternalInput").ap()
    d_convw = nc.dram_tensor("convw", [128, 64], F32, kind="ExternalInput").ap()
    d_convb = nc.dram_tensor("convb", [128, 16], F32, kind="ExternalInput").ap()
    d_wxt = nc.dram_tensor("W_xTp", [DI, 96], F32, kind="ExternalInput").ap()
    d_wdtt = nc.dram_tensor("W_dtT", [DTR, DI], F32, kind="ExternalInput").ap()
    d_bdt = nc.dram_tensor("b_dt", [128, 16], F32, kind="ExternalInput").ap()
    d_alog = nc.dram_tensor("A_log", [128, 256], F32, kind="ExternalInput").ap()
    d_dp = nc.dram_tensor("Dp", [128, 16], F32, kind="ExternalInput").ap()
    d_woutt = nc.dram_tensor("W_outT", [DI, D], F32, kind="ExternalInput").ap()
    d_lng = nc.dram_tensor("ln_g", [128, 8], F32, kind="ExternalInput").ap()
    d_lnb = nc.dram_tensor("ln_b", [128, 8], F32, kind="ExternalInput").ap()
    d_attnw = nc.dram_tensor("attn_w", [D, D], F32, kind="ExternalInput").ap()
    d_attnb = nc.dram_tensor("attn_b", [128, 8], F32, kind="ExternalInput").ap()
    d_w1 = [nc.dram_tensor("w1A", [D, DFF], F32, kind="ExternalInput").ap(),
            nc.dram_tensor("w1B", [D, DFF], F32, kind="ExternalInput").ap()]
    d_b1 = [nc.dram_tensor("b1A", [128, 32], F32, kind="ExternalInput").ap(),
            nc.dram_tensor("b1B", [128, 32], F32, kind="ExternalInput").ap()]
    d_w2 = [nc.dram_tensor("w2A", [DFF, D], F32, kind="ExternalInput").ap(),
            nc.dram_tensor("w2B", [DFF, D], F32, kind="ExternalInput").ap()]
    d_b2 = [nc.dram_tensor("b2A", [128, 8], F32, kind="ExternalInput").ap(),
            nc.dram_tensor("b2B", [128, 8], F32, kind="ExternalInput").ap()]
    d_nextw = nc.dram_tensor("next_w", [D, D], F32, kind="ExternalInput").ap()
    d_nextb = nc.dram_tensor("next_b", [128, 8], F32, kind="ExternalInput").ap()
    d_out = nc.dram_tensor("out", [SEQ, D], F32, kind="ExternalOutput").ap()

    d_dts = nc.dram_tensor("d_dts", [NCHUNK * 16 * 128, LC], F32).ap()
    d_xcs = nc.dram_tensor("d_xcs", [NCHUNK * 16 * 128, LC], F32).ap()
    d_winb = nc.dram_tensor("d_winb", [D, E2], BF16).ap()
    d_woutb = nc.dram_tensor("d_woutb", [DI, D], BF16).ap()
    d_w1b = [nc.dram_tensor("d_w1bA", [D, DFF], BF16).ap(),
             nc.dram_tensor("d_w1bB", [D, DFF], BF16).ap()]
    d_w2b = [nc.dram_tensor("d_w2bA", [DFF, D], BF16).ap(),
             nc.dram_tensor("d_w2bB", [DFF, D], BF16).ap()]
    d_attnbf = nc.dram_tensor("d_attnbf", [D, D], BF16).ap()
    d_nxwbf = nc.dram_tensor("d_nxwbf", [D, D], BF16).ap()

    em = Emitter()
    E = em.emit

    with contextlib.ExitStack() as _st:
        _A = _st.enter_context
        wxt_bf = _A(nc.sbuf_tensor("wxt_bf", [128, 16 * 96], BF16))
        wdtt_bf = _A(nc.sbuf_tensor("wdtt_bf", [DTR, DI], BF16))
        a_pos = _A(nc.sbuf_tensor("a_pos", [128, 256], F32))
        sel = _A(nc.sbuf_tensor("sel", [32, 32 * 128], BF16))
        identf = _A(nc.sbuf_tensor("identf", [128, 128], F32))
        identb = _A(nc.sbuf_tensor("identb", [128, 128], BF16))
        diagdp = _A(nc.sbuf_tensor("diagdp", [128, 16 * 128], BF16))
        convw_sb = _A(nc.sbuf_tensor("convw_sb", [128, 64], F32))
        dp_sb = _A(nc.sbuf_tensor("dp_sb", [128, 16], F32))
        negbdt = _A(nc.sbuf_tensor("negbdt", [128, 16], F32))
        convb_sb = _A(nc.sbuf_tensor("convb_sb", [128, 16], F32))
        biasp = _A(nc.sbuf_tensor("biasp", [128, 8], F32))
        nextb_sb = _A(nc.sbuf_tensor("nextb_sb", [128, 8], F32))
        b1_sb = _A(nc.sbuf_tensor("b1_sb", [128, 64], F32))
        b2_sb = _A(nc.sbuf_tensor("b2_sb", [128, 16], F32))
        lng_sb = _A(nc.sbuf_tensor("lng_sb", [128, 8], F32))
        ones_bf = _A(nc.sbuf_tensor("ones_bf", [128, 128], BF16))
        eps_sb = _A(nc.sbuf_tensor("eps_sb", [128, 1], F32))
        one_sb = _A(nc.sbuf_tensor("one_sb", [128, 1], F32))
        q_eng = {('e_' + e): _A(nc.semaphore('e_' + e))
                 for e in ('sync', 'scalar', 'vector', 'tensor', 'gpsimd')}
        dma_q = {dn: _A(nc.semaphore(dn)) for dn in DMA_SEMS}
        sems = dict(q_eng)
        sems.update(dma_q)

        with contextlib.ExitStack() as _st0:
            stage = _st0.enter_context(nc.sbuf_tensor("stage", [128, 8 * D], F32))
            attn_st = _st0.enter_context(nc.sbuf_tensor("attn_st", [128, 8 * D], BF16))
            lnb_bf = _st0.enter_context(nc.sbuf_tensor("lnb_bf", [128, 8], BF16))
            attnb_st = _st0.enter_context(nc.sbuf_tensor("attnb_st", [128, 8], F32))
            ps0 = _st0.enter_context(nc.psum_tensor("ps0", [128, 512], F32))

            for dst, key, src in [(a_pos, 'a_pos', d_alog), (negbdt, 'negbdt', d_bdt),
                                  (convb_sb, 'convb', d_convb), (nextb_sb, 'nextb', d_nextb),
                                  (lng_sb, 'lng', d_lng), (attnb_st, 'attnb', d_attnb),
                                  (convw_sb, 'convw', d_convw), (dp_sb, 'dp', d_dp)]:
                E('sync', lambda s, dst=dst, src=src: s.dma_start(dst[:, :], src),
                  writes=[key], dma_sem='d_su_s')
            E('sync', lambda s: s.dma_start(b1_sb[:, 0:32], d_b1[0]), writes=['b1'], dma_sem='d_su_s')
            E('sync', lambda s: s.dma_start(b1_sb[:, 32:64], d_b1[1]), writes=['b1'], dma_sem='d_su_s')
            E('sync', lambda s: s.dma_start(b2_sb[:, 0:8], d_b2[0]), writes=['b2'], dma_sem='d_su_s')
            E('sync', lambda s: s.dma_start(b2_sb[:, 8:16], d_b2[1]), writes=['b2'], dma_sem='d_su_s')
            E('sync', lambda s: s.dma_start(stage[:, 80:88], d_lnb), writes=['stage_t'], dma_sem='d_su_s')

            E('gpsimd', lambda g: g.dma_start(d_winb, d_wint), writes=['cv_win'], dma_sem='d_cv_win')
            for g_i in range(16):
                E('gpsimd', lambda g, g_i=g_i: g.dma_start(
                    wxt_bf[:, g_i * 96:(g_i + 1) * 96], d_wxt[g_i * 128:(g_i + 1) * 128, :]),
                  writes=['wxt'], dma_sem='d_su_g')
            E('gpsimd', lambda g: g.dma_start(wdtt_bf[:, :], d_wdtt), writes=['wdtt'], dma_sem='d_su_g')

            E('scalar', lambda sc: sc.activation(a_pos[:, :], a_pos[:, :], AF.Exp),
              reads=['a_pos'], writes=['a_pos'])
            E('vector', lambda v: v.tensor_scalar_mul(a_pos[:, :], a_pos[:, :], -1.0),
              reads=['a_pos'], writes=['a_pos'])
            E('vector', lambda v: v.tensor_copy(lnb_bf[:, :], stage[:, 80:88]),
              reads=['stage_t'], writes=['lnb_bf'])

            for t, k in ((identf, 'identf'), (identb, 'identb')):
                E('gpsimd', lambda g, t=t: g.memset(t[:, :], 1.0), writes=[k])
                E('gpsimd', lambda g, t=t: g.affine_select(
                    t[:, :], t[:, :], [[1, 128]], OP.is_equal, 0.0, base=0,
                    channel_multiplier=-1), reads=[k], writes=[k])
            E('gpsimd', lambda g: g.memset(ones_bf[:, :], 1.0), writes=['ones_bf'])
            E('gpsimd', lambda g: g.memset(eps_sb[:, :], LN_EPS), writes=['eps'])
            E('gpsimd', lambda g: g.memset(one_sb[:, :], 1.0), writes=['one'])
            E('vector', lambda v: v.memset(sel[:, :], 1.0), writes=['sel'])
            E('gpsimd', lambda g: g.affine_select(
                sel[:, :].rearrange("p (t j) -> p t j", j=128),
                sel[:, :].rearrange("p (t j) -> p t j", j=128),
                [[1, 32], [0, 128]], OP.is_equal, 0.0, base=0, channel_multiplier=-1),
              reads=['sel'], writes=['sel'])
            for g_i in range(16):
                E('scalar', lambda sc, g_i=g_i: sc.activation(
                    diagdp[:, g_i * 128:(g_i + 1) * 128], ones_bf[:, :], AF.Copy,
                    scale=dp_sb[:, g_i:g_i + 1]),
                  reads=['ones_bf', 'dp'], writes=['diagdp'])
            E('gpsimd', lambda g: g.affine_select(
                diagdp[:, :].rearrange("p (t f) -> p t f", f=128),
                diagdp[:, :].rearrange("p (t f) -> p t f", f=128),
                [[0, 16], [1, 128]], OP.is_equal, 0.0, base=0, channel_multiplier=-1),
              reads=['diagdp'], writes=['diagdp'])

            for dc in range(8):
                E('sync', lambda s, dc=dc: s.dma_start(
                    stage[:, dc * D:(dc + 1) * D], d_attnw[dc * 128:(dc + 1) * 128, :]),
                  writes=['stage', 'stage_t'], dma_sem='d_su_s')
            E('vector', lambda v: v.tensor_copy(attn_st[:, :], stage[:, :]),
              reads=['stage'], writes=['attn_st'])
            for ec in range(8):
                for dc in range(8):
                    E('tensor', lambda te, ec=ec, dc=dc: te.matmul(
                        ps0[:, ec:ec + 1],
                        attn_st[:, dc * D + ec * 128: dc * D + (ec + 1) * 128],
                        lnb_bf[:, dc:dc + 1],
                        start=(dc == 0), stop=(dc == 7)),
                      reads=['lnb_bf', 'attn_st'], writes=['ps0'], inc=(dc == 7))
            E('scalar', lambda sc: sc.copy(biasp[:, :], ps0[:, 0:8]),
              reads=['ps0'], writes=['biasp'])
            E('vector', lambda v: v.tensor_add(biasp[:, :], biasp[:, :], attnb_st[:, :]),
              reads=['biasp', 'attnb'], writes=['biasp'])
            for dc in range(8):
                E('vector', lambda v, dc=dc: v.tensor_scalar(
                    attn_st[:, dc * D:(dc + 1) * D], stage[:, dc * D:(dc + 1) * D],
                    lng_sb[:, dc:dc + 1], None, OP.mult),
                  reads=['stage', 'lng'], writes=['attn_st'])
            E('sync', lambda s: s.dma_start(
                d_attnbf.rearrange("(dc p) e -> p dc e", p=128),
                attn_st[:, :].rearrange("p (dc e) -> p dc e", dc=8)),
              reads=['attn_st'], writes=['cv_at'], dma_sem='d_su_s')

            with nc.Block() as blk0:
                @blk0.sync
                def _(s):
                    em.run('sync', s, sems)

                @blk0.scalar
                def _(s):
                    em.run('scalar', s, sems)

                @blk0.vector
                def _(s):
                    em.run('vector', s, sems)

                @blk0.tensor
                def _(s):
                    em.run('tensor', s, sems)

                @blk0.gpsimd
                def _(s):
                    em.run('gpsimd', s, sems)
            em.ops = []
        em.fence_all()

        with contextlib.ExitStack() as _st1:
            _M = _st1.enter_context
            xn_fm = _M(nc.sbuf_tensor("xn_fm", [128, 2 * 8 * LC], F32))
            xn_fmb = _M(nc.sbuf_tensor("xn_fmb", [128, 8 * LC], BF16))
            ln2_fmb = _M(nc.sbuf_tensor("ln2_fmb", [128, 2 * 8 * LC], BF16))
            sz_bf = _M(nc.sbuf_tensor("sz_bf", [128, 2 * 16 * LC], BF16))
            gated = _M(nc.sbuf_tensor("gated", [128, 2 * 16 * LC], BF16))
            bc_bf = _M(nc.sbuf_tensor("bc_bf", [32, 2 * LC], BF16))
            dtlow_bf = _M(nc.sbuf_tensor("dtlow_bf", [DTR, LC], BF16))
            s_fm = _M(nc.sbuf_tensor("s_fm", [128, 8 * LC], F32))
            hcarry = _M(nc.sbuf_tensor("hcarry", [128, 256], F32))
            halo_sb = _M(nc.sbuf_tensor("halo_sb", [128, 16 * 3], F32))
            stats = _M(nc.sbuf_tensor("stats", [128, 32], F32))
            ball = _M(nc.sbuf_tensor("ball", [128, 16 * LC], BF16))
            call = _M(nc.sbuf_tensor("call", [128, 16 * LC], BF16))
            da0 = _M(nc.sbuf_tensor("da0", [128, NH * (LC + 1)], F32))
            da1 = _M(nc.sbuf_tensor("da1", [128, NH * (LC + 1)], F32))
            db0 = _M(nc.sbuf_tensor("db0", [128, NH * (LC + 1)], F32))
            db1 = _M(nc.sbuf_tensor("db1", [128, NH * (LC + 1)], F32))
            da_p = [da0, da1]
            dbx_p = [db0, db1]
            hc_p = [db0[:, :].bitcast(BF16)[:, 0:NH * LC],
                    db1[:, :].bitcast(BF16)[:, 0:NH * LC]]
            dtin = _M(nc.sbuf_tensor("dtin", [128, 4 * LC], F32))
            xcin = _M(nc.sbuf_tensor("xcin", [128, 4 * LC], F32))
            u_r = _M(nc.sbuf_tensor("u_r", [128, 2 * LC], F32))
            xcbf_r = _M(nc.sbuf_tensor("xcbf_r", [128, 2 * LC], BF16))
            xdbl_sb = _M(nc.sbuf_tensor("xdbl_sb", [96, LC], F32))
            pool = _M(nc.sbuf_tensor("pool", [128, 26624], BF16))

            def pv_f32(off_b, nelem):
                k = off_b // 2
                return pool[:, k:k + 2 * nelem].bitcast(F32)

            def pv_bf(off_b, nelem):
                k = off_b // 2
                return pool[:, k:k + nelem]

            x_tm = pv_f32(0, NTT * D)
            xn_tm = pv_f32(8192, NTT * D)
            ln2_tm = pv_f32(16384, NTT * D)
            trash = pv_f32(24576, D)
            winq = [pv_bf(0, 8 * 512), pv_bf(8192, 8 * 512)]
            xcfull = pv_f32(16384, 16 * (LC + 3))
            xcv_r = [pv_f32(33536, LC), pv_f32(34560, LC)]
            dtr_r = [pv_f32(35584, LC), pv_f32(36608, LC)]
            xcb_r = [pv_bf(37632, LC), pv_bf(38144, LC)]
            woutb = pv_bf(0, 16 * 1024)
            w1r = [pv_bf(0, 8 * 512), pv_bf(8192, 8 * 512)]
            w2r = [pv_bf(16384, 4 * 1024), pv_bf(24576, 4 * 1024)]
            h1r = [pv_bf(32768, 4 * LC), pv_bf(34816, 4 * LC)]
            xa_bf = pv_bf(36864, 8 * LC)
            y2_bf = pv_bf(40960, 8 * LC)
            gg = pv_bf(45056, 8 * LC)
            attn_r = [pv_bf(49152, 8 * 128), pv_bf(51200, 8 * 128)]
            next_r = attn_r
            xre = pv_f32(0, NTT * D)
            out_tm = pv_f32(8192, NTT * D)
            comb = pv_f32(16384, 8 * LC)

            POOL_REGION = {
                'x_tm': (0, 8192), 'xn_tm': (8192, 16384), 'ln2_tm': (16384, 24576),
                'trash': (24576, 28672),
                'winq0': (0, 8192), 'winq1': (8192, 16384), 'xcfull': (16384, 33152),
                'xcv0': (33536, 34560), 'xcv1': (34560, 35584),
                'dtr0': (35584, 36608), 'dtr1': (36608, 37632),
                'xcb0': (37632, 38144), 'xcb1': (38144, 38656),
                'woutb': (0, 32768),
                'w1b0': (0, 8192), 'w1b1': (8192, 16384),
                'w2b0': (16384, 24576), 'w2b1': (24576, 32768),
                'h10': (32768, 34816), 'h11': (34816, 36864),
                'xa': (36864, 40960), 'y2': (40960, 45056),
                'gg': (45056, 49152),
                'at0': (49152, 51200), 'at1': (51200, 53248),
                'xre': (0, 8192), 'out_tm': (8192, 16384), 'comb': (16384, 24576),
            }

            def povl(key):
                a0, a1 = POOL_REGION[key]
                return [k for k, (b0, b1) in POOL_REGION.items()
                        if k != key and b0 < a1 and a0 < b1]

            psA = _M(nc.psum_tensor("psA", [128, 512], F32))
            psB = _M(nc.psum_tensor("psB", [128, 512], F32))
            psY0 = _M(nc.psum_tensor("psY0", [128, 512], F32))
            psY1 = _M(nc.psum_tensor("psY1", [128, 512], F32))
            psY_p = [(psY0, 'psY0'), (psY1, 'psY1')]
            psY2 = _M(nc.psum_tensor("psY2", [128, 8 * LC], F32))
            GPS = [(psA, 'psA'), (psB, 'psB')]
            gps_i = [0]

            def P():
                ps, pn = GPS[gps_i[0] % 2]
                gps_i[0] += 1
                return ps, pn

            def phase_A(c):
                L = []
                par = c % 2
                t0 = c * LC
                xnp = xn_fm[:, par * 8 * LC:(par + 1) * 8 * LC]
                ln2p = ln2_fmb[:, par * 8 * LC:(par + 1) * 8 * LC]
                xnk, ln2k = f'xn{par}', f'ln2{par}'
                TH(L, 'sync', lambda s, t0=t0: s.dma_start(
                    x_tm[:, :].rearrange("p (tt d) -> p tt d", tt=NTT),
                    dx[t0:t0 + LC, :].rearrange("(tt p) d -> p tt d", p=128)),
                   writes=['x_tm'], dma_sem='d_xin', war=povl('x_tm'), est=7000)
                for src, dstk, dst in ((x_tm, 'xn_tm', xn_tm), (xn_tm, 'ln2_tm', ln2_tm)):
                    srck = 'x_tm' if src is x_tm else 'xn_tm'
                    for tt in range(NTT):
                        for hh in range(2):
                            TH(L, 'vector', lambda v, tt=tt, hh=hh, src=src: v.bn_stats(
                                stats[:, 12 * tt + 6 * hh:12 * tt + 6 * hh + 6],
                                src[:, tt * D + hh * 512: tt * D + (hh + 1) * 512]),
                               reads=[srck], writes=['stats'], est=680)
                        TH(L, 'vector', lambda v, tt=tt: v.bn_aggr(
                            stats[:, 24 + 2 * tt:26 + 2 * tt], stats[:, 12 * tt:12 * tt + 12]),
                           reads=['stats'], writes=['stats'], est=300)
                    TH(L, 'scalar', lambda sc: sc.activation(
                        stats[:, 25:28:2], stats[:, 25:28:2], AF.Ln,
                        bias=eps_sb[:, 0:1]),
                       reads=['stats', 'eps'], writes=['stats'])
                    TH(L, 'scalar', lambda sc: sc.activation(
                        stats[:, 25:28:2], stats[:, 25:28:2], AF.Exp,
                        scale=-0.5),
                       reads=['stats'], writes=['stats'])
                    for tt in range(NTT):
                        TH(L, 'vector', lambda v, tt=tt, src=src, dst=dst: v.tensor_scalar(
                            dst[:, tt * D:(tt + 1) * D], src[:, tt * D:(tt + 1) * D],
                            stats[:, 24 + 2 * tt:25 + 2 * tt], stats[:, 25 + 2 * tt:26 + 2 * tt],
                            OP.subtract, OP.mult),
                           reads=[srck, 'stats'], writes=[dstk],
                           war=(povl(dstk) if tt == 0 else ()))
                for src, srck, dst, dstk in ((xn_tm, 'xn_tm', xnp, xnk),
                                             (ln2_tm, 'ln2_tm', ln2p, ln2k)):
                    for tt in range(NTT):
                        for dc in range(8):
                            ps, pn = P()
                            TH(L, 'tensor', lambda te, tt=tt, dc=dc, ps=ps, src=src: te.transpose(
                                ps[:, 0:128], src[:, tt * D + dc * 128: tt * D + (dc + 1) * 128],
                                identf[:, :]),
                               reads=[srck, 'identf'], writes=[pn])
                            TH(L, 'scalar', lambda sc, tt=tt, dc=dc, ps=ps, dst=dst: sc.copy(
                                dst[:, dc * LC + tt * 128: dc * LC + tt * 128 + 128], ps[:, 0:128]),
                               reads=[pn], writes=[dstk])
                TH(L, 'vector', lambda v, xnp=xnp: v.tensor_copy(xn_fmb[:, :], xnp[:, :]),
                   reads=[xnk], writes=['xn_fmb'])
                return L

            def phase_B(c):
                L = []
                par = c % 2
                szp = sz_bf[:, par * 16 * LC:(par + 1) * 16 * LC]
                szk = f'sz{par}'
                bck = f'bc{par}'
                TH(L, 'vector', lambda v: v.tensor_copy(
                    xcfull[:, :].rearrange("p (g e) -> p g e", e=LC + 3)[:, :, 0:3],
                    halo_sb[:, :].rearrange("p (g e) -> p g e", e=3)),
                   reads=['halo'], writes=['xcfull'], war=povl('xcfull'))
                for q in range(8):
                    wq = winq[q % 2]
                    wqn = f'winq{q % 2}'
                    TH(L, 'gpsimd', lambda s, q=q, wq=wq: s.dma_start(
                        wq[:, :].rearrange("p (dc e) -> p dc e", dc=8),
                        d_winb.rearrange("(dc p) e -> p dc e", p=128)[:, :, q * 512:(q + 1) * 512]),
                       reads=['cv_win'], writes=[wqn], dma_sem=f'd_win{q % 2}',
                       war=(povl(wqn) if q < 2 else ()), est=9000)
                    for el in range(4):
                        ec = q * 4 + el
                        ps, pn = P()
                        for dc in range(8):
                            TH(L, 'tensor', lambda te, wq=wq, el=el, dc=dc, ps=ps: te.matmul(
                                ps[:, 0:LC],
                                wq[:, dc * 512 + el * 128: dc * 512 + (el + 1) * 128],
                                xn_fmb[:, dc * LC:(dc + 1) * LC],
                                start=(dc == 0), stop=(dc == 7)),
                               reads=[wqn, 'xn_fmb'], writes=[pn], inc=(dc == 7))
                        if ec < 16:
                            TH(L, 'scalar', lambda sc, ec=ec, ps=ps: sc.copy(
                                xcfull[:, ec * (LC + 3) + 3: ec * (LC + 3) + 3 + LC],
                                ps[:, 0:LC]),
                               reads=[pn], writes=['xcfull'])
                        else:
                            TH(L, 'scalar', lambda sc, ec=ec, ps=ps, szp=szp: sc.activation(
                                szp[:, (ec - 16) * LC:(ec - 15) * LC], ps[:, 0:LC], AF.Silu),
                               reads=[pn], writes=[szk], tab='s')
                xdblps = None
                for g_i in range(16):
                    base = g_i * (LC + 3)
                    xr = xcv_r[g_i % 2]
                    xrk = f'xcv{g_i % 2}'
                    xb = xcb_r[g_i % 2]
                    xbk = f'xcb{g_i % 2}'
                    TH(L, 'vector', lambda v, base=base, xr=xr, g_i=g_i: v.tensor_scalar(
                        xr[:, :], xcfull[:, base: base + LC],
                        convw_sb[:, g_i * 4: g_i * 4 + 1], None, OP.mult),
                       reads=['xcfull', 'convw'], writes=[xrk],
                       war=(povl(xrk) if g_i < 2 else ()))
                    for k in range(1, 4):
                        TH(L, 'vector', lambda v, base=base, xr=xr, g_i=g_i, k=k:
                           v.scalar_tensor_tensor(
                               xr[:, :], xcfull[:, base + k: base + k + LC],
                               convw_sb[:, g_i * 4 + k: g_i * 4 + k + 1],
                               xr[:, :], OP.mult, OP.add),
                           reads=['xcfull', 'convw', xrk], writes=[xrk])
                    TH(L, 'scalar', lambda sc, xr=xr, g_i=g_i: sc.activation(
                        xr[:, :], xr[:, :], AF.Silu, bias=convb_sb[:, g_i:g_i + 1]),
                       reads=[xrk, 'convb'], writes=[xrk], tab='s')
                    TH(L, 'scalar', lambda sc, xr=xr, xb=xb: sc.copy(xb[:, :], xr[:, :]),
                       reads=[xrk], writes=[xbk],
                       war=(povl(xbk) if g_i < 2 else ()))
                    TH(L, 'sync', lambda s, xr=xr, c=c, g_i=g_i: s.dma_start(
                        d_xcs[(c * 16 + g_i) * 128:(c * 16 + g_i + 1) * 128, :], xr[:, :]),
                       reads=[xrk], writes=[f'dxc{g_i}'], dma_sem=f'd_xco{g_i % 2}', est=3200)
                    psx, pxn = (xdblps if g_i % 4 != 0 else P())
                    if g_i % 4 == 0:
                        xdblps = (psx, pxn)
                    TH(L, 'tensor', lambda te, g_i=g_i, xb=xb, psx=psx: te.matmul(
                        psx[0:96, 0:LC], wxt_bf[:, g_i * 96:(g_i + 1) * 96],
                        xb[:, :], start=(g_i % 4 == 0), stop=(g_i % 4 == 3)),
                       reads=['wxt', xbk], writes=[pxn], inc=True)
                    if g_i % 4 == 3:
                        if g_i == 3:
                            TH(L, 'vector', lambda v, psx=psx: v.tensor_copy(
                                xdbl_sb[:, :], psx[0:96, 0:LC]),
                               reads=[pxn], writes=['xdbl'])
                        else:
                            TH(L, 'vector', lambda v, psx=psx: v.tensor_add(
                                xdbl_sb[:, :], psx[0:96, 0:LC], xdbl_sb[:, :]),
                               reads=[pxn, 'xdbl'], writes=['xdbl'])
                TH(L, 'vector', lambda v: v.tensor_copy(
                    halo_sb[:, :].rearrange("p (g e) -> p g e", e=3),
                    xcfull[:, :].rearrange("p (g e) -> p g e", e=LC + 3)[:, :, LC:LC + 3]),
                   reads=['xcfull'], writes=['halo'])
                TH(L, 'scalar', lambda sc, par=par: sc.copy(
                    bc_bf[:, par * LC:(par + 1) * LC], xdbl_sb[0:32, :]),
                   reads=['xdbl'], writes=[bck])
                TH(L, 'scalar', lambda sc: sc.copy(dtlow_bf[0:32, :], xdbl_sb[32:64, :]),
                   reads=['xdbl'], writes=['dtlow'])
                TH(L, 'scalar', lambda sc: sc.copy(dtlow_bf[32:64, :], xdbl_sb[64:96, :]),
                   reads=['xdbl'], writes=['dtlow'])
                for g_i in range(16):
                    ps, pn = P()
                    dr = dtr_r[g_i % 2]
                    drk = f'dtr{g_i % 2}'
                    TH(L, 'tensor', lambda te, g_i=g_i, ps=ps: te.matmul(
                        ps[:, 0:LC], wdtt_bf[:, g_i * 128:(g_i + 1) * 128],
                        dtlow_bf[:, :], start=True, stop=True),
                       reads=['wdtt', 'dtlow'], writes=[pn])
                    TH(L, 'scalar', lambda sc, g_i=g_i, ps=ps, dr=dr: sc.activation(
                        dr[:, :], ps[:, 0:LC], AF.Exp,
                        bias=negbdt[:, g_i:g_i + 1]),
                       reads=[pn, 'negbdt'], writes=[drk],
                       war=(povl(drk) if g_i < 2 else ()))
                    TH(L, 'scalar', lambda sc, dr=dr: sc.activation(
                        dr[:, :], dr[:, :], AF.Ln, bias=one_sb[:, 0:1]),
                       reads=[drk, 'one'], writes=[drk])
                    TH(L, 'sync', lambda s, dr=dr, c=c, g_i=g_i: s.dma_start(
                        d_dts[(c * 16 + g_i) * 128:(c * 16 + g_i + 1) * 128, :], dr[:, :]),
                       reads=[drk], writes=[f'ddt{g_i}'], dma_sem=f'd_dto{g_i % 2}', est=3200)
                return L

            def _c_slice_in(c, g_i):
                L = []
                r = g_i % 4
                TH(L, 'sync', lambda s, c=c, g_i=g_i, r=r: s.dma_start(
                    dtin[:, r * LC:(r + 1) * LC],
                    d_dts[(c * 16 + g_i) * 128:(c * 16 + g_i + 1) * 128, :]),
                   reads=[f'ddt{g_i}'], writes=[f'dti{r}'], dma_sem=f'd_dti{r}', est=3200)
                TH(L, 'sync', lambda s, c=c, g_i=g_i, r=r: s.dma_start(
                    xcin[:, r * LC:(r + 1) * LC],
                    d_xcs[(c * 16 + g_i) * 128:(c * 16 + g_i + 1) * 128, :]),
                   reads=[f'dxc{g_i}'], writes=[f'xci{r}'], dma_sem=f'd_xci{r}', est=3200)
                return L

            def phase_C_prologue(c):
                L = []
                par = c % 2
                bck = f'bc{par}'
                bcs = bc_bf[:, par * LC:(par + 1) * LC]
                for n in range(16):
                    ps, pn = P()
                    TH(L, 'tensor', lambda te, n=n, ps=ps, bcs=bcs: te.matmul(
                        ps[:, 0:LC], sel[:, n * 128:(n + 1) * 128], bcs,
                        start=True, stop=True),
                       reads=['sel', bck], writes=[pn])
                    TH(L, 'scalar', lambda sc, n=n, ps=ps: sc.copy(
                        ball[:, n * LC:(n + 1) * LC], ps[:, 0:LC]),
                       reads=[pn], writes=['ball'])
                    ps, pn = P()
                    TH(L, 'tensor', lambda te, n=n, ps=ps, bcs=bcs: te.matmul(
                        ps[:, 0:LC], sel[:, (16 + n) * 128:(17 + n) * 128], bcs,
                        start=True, stop=True),
                       reads=['sel', bck], writes=[pn])
                    TH(L, 'scalar', lambda sc, n=n, ps=ps: sc.copy(
                        call[:, n * LC:(n + 1) * LC], ps[:, 0:LC]),
                       reads=[pn], writes=['call'])
                for g_i in range(4):
                    L += _c_slice_in(c, g_i)
                return L

            def _c_gate(c, g_i):
                L = []
                par = c % 2
                szk = f'sz{par}'
                gtk = f'gated{par}'
                gtp = gated[:, par * 16 * LC:(par + 1) * 16 * LC]
                psy, psyk = psY_p[g_i % 2]
                TH(L, 'vector', lambda v, g_i=g_i, gtp=gtp, par=par, psy=psy: v.tensor_mul(
                    gtp[:, g_i * LC:(g_i + 1) * LC], psy[:, 0:LC],
                    sz_bf[:, (par * 16 + g_i) * LC:(par * 16 + g_i + 1) * LC]),
                   reads=[psyk, szk], writes=[gtk])
                return L

            def phase_C_iter(c, g_i):
                L = []
                par = c % 2
                r = g_i % 4
                r2 = g_i % 2
                szk = f'sz{par}'
                gtk = f'gated{par}'
                gtp = gated[:, par * 16 * LC:(par + 1) * 16 * LC]
                dts = dtin[:, r * LC:(r + 1) * LC]
                xcs = xcin[:, r * LC:(r + 1) * LC]
                us = u_r[:, r2 * LC:(r2 + 1) * LC]
                xbs = xcbf_r[:, r2 * LC:(r2 + 1) * LC]
                TH(L, 'vector', lambda v, dts=dts, xcs=xcs, us=us: v.tensor_mul(
                    us, dts, xcs),
                   reads=[f'dti{r}', f'xci{r}'], writes=[f'u{r2}'])
                TH(L, 'scalar', lambda sc, xcs=xcs, xbs=xbs: sc.copy(xbs, xcs),
                   reads=[f'xci{r}'], writes=[f'xcb2{r2}'])
                for half in range(2):
                    p_i = (g_i * 2 + half) % 2
                    da = da_p[p_i]
                    dbx = dbx_p[p_i]
                    hc = hc_p[p_i]
                    dak = f'da{p_i}'
                    dbk = f'dbx{p_i}'
                    TH(L, 'gpsimd', lambda g, g_i=g_i, half=half, dbx=dbx: g.tensor_copy(
                        dbx[:, 0::LC + 1],
                        hcarry[:, g_i * 16 + half * NH: g_i * 16 + half * NH + NH]),
                       reads=['hcarry'], writes=[dbk], est=700)
                    for i in range(NH):
                        n = half * NH + i
                        TH(L, 'scalar', lambda sc, g_i=g_i, i=i, n=n, da=da, dts=dts: sc.activation(
                            da[:, i * (LC + 1) + 1:(i + 1) * (LC + 1)], dts, AF.Exp,
                            scale=a_pos[:, g_i * 16 + n: g_i * 16 + n + 1]),
                           reads=[f'dti{r}', 'a_pos'], writes=[dak], est=650)
                    for i in range(NH):
                        n = half * NH + i
                        TH(L, 'vector', lambda v, i=i, n=n, dbx=dbx, us=us: v.tensor_tensor(
                            dbx[:, i * (LC + 1) + 1:(i + 1) * (LC + 1)], us,
                            ball[:, n * LC:(n + 1) * LC], OP.mult),
                           reads=[f'u{r2}', 'ball'], writes=[dbk], est=450)
                    TH(L, 'vector', lambda v, da=da, dbx=dbx: v.tensor_tensor_scan(
                        dbx[:, 0:NH * (LC + 1)],
                        da[:, 0:NH * (LC + 1)],
                        dbx[:, 0:NH * (LC + 1)],
                        0.0, OP.mult, OP.add),
                       reads=[dak, dbk], writes=[dbk], est=4600)
                    TH(L, 'gpsimd', lambda g, g_i=g_i, half=half, dbx=dbx: g.tensor_copy(
                        hcarry[:, g_i * 16 + half * NH: g_i * 16 + half * NH + NH],
                        dbx[:, LC::LC + 1]),
                       reads=[dbk], writes=['hcarry'], est=700)
                    TH(L, 'gpsimd', lambda g, half=half, dbx=dbx, hc=hc: g.tensor_tensor(
                        hc[:, :].rearrange("p (n l) -> p n l", n=NH),
                        dbx[:, :].rearrange("p (n l) -> p n l", l=LC + 1)[:, :, 1:LC + 1],
                        call[:, half * NH * LC:(half + 1) * NH * LC]
                        .rearrange("p (n l) -> p n l", n=NH), OP.mult),
                       reads=[dbk, 'call'], writes=[dbk], est=4300)
                    psy, psyk = psY_p[g_i % 2]
                    for i in range(NH):
                        TH(L, 'tensor', lambda te, i=i, half=half, hc=hc, psy=psy: te.matmul(
                            psy[:, 0:LC], identb[:, :], hc[:, i * LC:(i + 1) * LC],
                            start=(half == 0 and i == 0), stop=False),
                           reads=[dbk, 'identb'], writes=[psyk], inc=(i == NH - 1))
                psy, psyk = psY_p[g_i % 2]
                TH(L, 'tensor', lambda te, g_i=g_i, xbs=xbs, psy=psy: te.matmul(
                    psy[:, 0:LC], diagdp[:, g_i * 128:(g_i + 1) * 128],
                    xbs, start=False, stop=True),
                   reads=['diagdp', f'xcb2{r2}'], writes=[psyk])
                if g_i >= 1:
                    L += _c_gate(c, g_i - 1)
                if g_i == 15:
                    L += _c_gate(c, 15)
                if g_i + 4 < 16:
                    L += _c_slice_in(c, g_i + 4)
                return L

            def phase_D(c):
                L = []
                par = c % 2
                gtk = f'gated{par}'
                gtp = gated[:, par * 16 * LC:(par + 1) * 16 * LC]
                TH(L, 'gpsimd', lambda s: s.dma_start(
                    woutb[:, :].rearrange("p (gi e) -> p gi e", gi=16),
                    d_woutb.rearrange("(gi p) e -> p gi e", p=128)),
                   reads=['cv_wout'], writes=['woutb'], dma_sem='d_wout', war=povl('woutb'), est=26000)
                for m in range(8):
                    ps, pn = P()
                    for g_i in range(16):
                        TH(L, 'tensor', lambda te, m=m, g_i=g_i, ps=ps, gtp=gtp: te.matmul(
                            ps[:, 0:LC],
                            woutb[:, g_i * 1024 + m * 128: g_i * 1024 + (m + 1) * 128],
                            gtp[:, g_i * LC:(g_i + 1) * LC],
                            start=(g_i == 0), stop=(g_i == 15)),
                           reads=['woutb', gtk], writes=[pn], inc=(g_i == 15))
                    TH(L, 'scalar', lambda sc, m=m, ps=ps: sc.copy(
                        s_fm[:, m * LC:(m + 1) * LC], ps[:, 0:LC]),
                       reads=[pn], writes=['s_fm'])
                return L

            def phase_E(c):
                L = []
                par = c % 2
                exp_i = 0 if c * LC < SEQ // 2 else 1
                xnk, ln2k = f'xn{par}', f'ln2{par}'
                xnp = xn_fm[:, par * 8 * LC:(par + 1) * 8 * LC]
                ln2p = ln2_fmb[:, par * 8 * LC:(par + 1) * 8 * LC]

                def w1_load(fb):
                    TH(L, 'gpsimd', lambda s, fb=fb, exp_i=exp_i: s.dma_start(
                        w1r[fb % 2][:, :].rearrange("p (dc f) -> p dc f", dc=8),
                        d_w1b[exp_i].rearrange("(dc p) f -> p dc f", p=128)
                        [:, :, fb * 512:(fb + 1) * 512]),
                       reads=['cv_w1'], writes=[f'w1b{fb % 2}'], dma_sem=f'd_w1{fb % 2}',
                       war=(povl(f'w1b{fb % 2}') if fb < 2 else ()), est=9000)

                def w2_load(fb):
                    TH(L, 'gpsimd', lambda s, fb=fb, exp_i=exp_i: s.dma_start(
                        w2r[fb % 2][:, :].rearrange("p (fp e) -> p fp e", fp=4),
                        d_w2b[exp_i].rearrange("(fq p) e -> p fq e", p=128)
                        [:, fb * 4:(fb + 1) * 4, :]),
                       reads=['cv_w2'], writes=[f'w2b{fb % 2}'], dma_sem=f'd_w2{fb % 2}',
                       war=(povl(f'w2b{fb % 2}') if fb < 2 else ()), est=9000)

                for m in range(8):
                    ar = attn_r[m % 2]
                    ark = f'at{m % 2}'
                    TH(L, 'gpsimd', lambda s, m=m, ar=ar: s.dma_start(
                        ar[:, :].rearrange("p (dc e) -> p dc e", dc=8),
                        d_attnbf.rearrange("(dc p) e -> p dc e", p=128)
                        [:, :, m * 128:(m + 1) * 128]),
                       reads=['cv_at'], writes=[ark], dma_sem=f'd_at{m % 2}',
                       war=(povl(ark) if m < 2 else ()))
                    ps, pn = P()
                    for dc in range(8):
                        TH(L, 'tensor', lambda te, m=m, dc=dc, ps=ps, ar=ar, ln2p=ln2p: te.matmul(
                            ps[:, 0:LC], ar[:, dc * 128:(dc + 1) * 128],
                            ln2p[:, dc * LC:(dc + 1) * LC],
                            start=(dc == 0), stop=(dc == 7)),
                           reads=[ark, ln2k], writes=[pn], inc=(dc == 7))
                    TH(L, 'vector', lambda v, m=m, ps=ps, xnp=xnp: v.scalar_tensor_tensor(
                        xa_bf[:, m * LC:(m + 1) * LC], ps[:, 0:LC],
                        biasp[:, m:m + 1], xnp[:, m * LC:(m + 1) * LC],
                        OP.add, OP.add),
                       reads=[pn, 'biasp', xnk], writes=['xa'],
                       war=(povl('xa') if m == 0 else ()))
                w1_load(0)
                w1_load(1)
                w2_load(0)
                for fb in range(8):
                    w1b = w1r[fb % 2]
                    w1k = f'w1b{fb % 2}'
                    w2b = w2r[fb % 2]
                    w2k = f'w2b{fb % 2}'
                    h1b = h1r[fb % 2]
                    h1k = f'h1{fb % 2}'
                    if fb + 1 < 8:
                        w2_load(fb + 1)
                    for fc in range(4):
                        ps, pn = P()
                        for dc in range(8):
                            TH(L, 'tensor', lambda te, fc=fc, dc=dc, ps=ps, w1b=w1b: te.matmul(
                                ps[:, 0:LC],
                                w1b[:, dc * 512 + fc * 128: dc * 512 + (fc + 1) * 128],
                                xa_bf[:, dc * LC:(dc + 1) * LC],
                                start=(dc == 0), stop=(dc == 7)),
                               reads=[w1k, 'xa'], writes=[pn], inc=(dc == 7))
                        TH(L, 'scalar', lambda sc, fb=fb, fc=fc, ps=ps, exp_i=exp_i, h1b=h1b: sc.activation(
                            h1b[:, fc * LC:(fc + 1) * LC], ps[:, 0:LC], AF.Gelu,
                            bias=b1_sb[:, exp_i * 32 + fb * 4 + fc: exp_i * 32 + fb * 4 + fc + 1]),
                           reads=[pn, 'b1'], writes=[h1k],
                           war=(povl(h1k) if fb < 2 and fc == 0 else ()), tab='g')
                    if fb + 2 < 8:
                        w1_load(fb + 2)
                    for m in range(8):
                        for fp in range(4):
                            TH(L, 'tensor', lambda te, fb=fb, m=m, fp=fp, w2b=w2b, h1b=h1b: te.matmul(
                                psY2[:, m * LC:(m + 1) * LC],
                                w2b[:, fp * 1024 + m * 128: fp * 1024 + (m + 1) * 128],
                                h1b[:, fp * LC:(fp + 1) * LC],
                                start=(fb == 0 and fp == 0 and m % 2 == 0),
                                stop=(fb == 7 and fp == 3 and m % 2 == 1)),
                               reads=[w2k, h1k], writes=['psY2'],
                               inc=(m == 7 and fp == 3))
                for m in range(8):
                    TH(L, 'scalar', lambda sc, m=m, exp_i=exp_i: sc.activation(
                        y2_bf[:, m * LC:(m + 1) * LC], psY2[:, m * LC:(m + 1) * LC],
                        AF.Identity, bias=b2_sb[:, exp_i * 8 + m: exp_i * 8 + m + 1]),
                       reads=['psY2', 'b2'], writes=['y2'],
                       war=(povl('y2') if m == 0 else ()))
                for m in range(8):
                    nr = next_r[m % 2]
                    nrk = f'at{m % 2}'
                    TH(L, 'gpsimd', lambda s, m=m, nr=nr: s.dma_start(
                        nr[:, :].rearrange("p (dc e) -> p dc e", dc=8),
                        d_nxwbf.rearrange("(dc p) e -> p dc e", p=128)
                        [:, :, m * 128:(m + 1) * 128]),
                       reads=['cv_nx'], writes=[nrk], dma_sem=f'd_nx{m % 2}',
                       war=(povl(nrk) if m < 2 else ()))
                    ps, pn = P()
                    for dc in range(8):
                        TH(L, 'tensor', lambda te, m=m, dc=dc, ps=ps, nr=nr: te.matmul(
                            ps[:, 0:LC], nr[:, dc * 128:(dc + 1) * 128],
                            y2_bf[:, dc * LC:(dc + 1) * LC],
                            start=(dc == 0), stop=(dc == 7)),
                           reads=[nrk, 'y2'], writes=[pn], inc=(dc == 7))
                    TH(L, 'scalar', lambda sc, m=m, ps=ps: sc.activation(
                        gg[:, m * LC:(m + 1) * LC], ps[:, 0:LC], AF.Gelu,
                        bias=nextb_sb[:, m:m + 1]),
                       reads=[pn, 'nextb'], writes=['gg'],
                       war=(povl('gg') if m == 0 else ()), tab='g')
                TH(L, 'scalar', lambda sc: sc.activation(gg[:, :], gg[:, :], AF.Silu),
                   reads=['gg'], writes=['gg'], tab='s')
                return L

            def phase_F(c):
                L = []
                t0 = c * LC
                TH(L, 'sync', lambda s, t0=t0: s.dma_start(
                    xre[:, :].rearrange("p (tt d) -> p tt d", tt=NTT),
                    dx[t0:t0 + LC, :].rearrange("(tt p) d -> p tt d", p=128)),
                   writes=['xre'], dma_sem='d_xre', war=povl('xre'), est=7000)
                TH(L, 'vector', lambda v: v.tensor_mul(comb[:, :], s_fm[:, :], gg[:, :]),
                   reads=['s_fm', 'gg'], writes=['comb'], war=povl('comb'), est=2400)
                for i, (m, tt) in enumerate([(m, tt) for tt in range(NTT) for m in range(8)]):
                    ps, pn = P()
                    TH(L, 'tensor', lambda te, m=m, tt=tt, ps=ps: te.transpose(
                        ps[:, 0:128], comb[:, m * LC + tt * 128: m * LC + tt * 128 + 128],
                        identf[:, :]),
                       reads=['comb', 'identf'], writes=[pn])
                    TH(L, 'vector', lambda v, m=m, tt=tt, ps=ps: v.tensor_add(
                        out_tm[:, tt * D + m * 128: tt * D + (m + 1) * 128],
                        ps[:, 0:128],
                        xre[:, tt * D + m * 128: tt * D + (m + 1) * 128]),
                       reads=[pn, 'xre'], writes=['out_tm'],
                       war=(povl('out_tm') if i == 0 else ()))
                TH(L, 'sync', lambda s, t0=t0: s.dma_start(
                    d_out[t0:t0 + LC, :].rearrange("(tt p) d -> p tt d", p=128),
                    out_tm[:, :].rearrange("p (tt d) -> p tt d", tt=NTT)),
                   reads=['out_tm'], dma_sem='d_xout', est=7000)
                return L

            def emit_list(L):
                for eng, build, kw in L:
                    E(eng, build, **kw)

            prog = []
            TH(prog, 'gpsimd', lambda g: g.dma_start(d_woutb, d_woutt),
               writes=['cv_wout'], dma_sem='d_cv_wout', est=12000)
            for i in range(2):
                TH(prog, 'gpsimd', lambda g, i=i: g.dma_start(d_w1b[i], d_w1[i]),
                   writes=['cv_w1'], dma_sem='d_cv_w1', est=30000)
                TH(prog, 'gpsimd', lambda g, i=i: g.dma_start(d_w2b[i], d_w2[i]),
                   writes=['cv_w2'], dma_sem='d_cv_w2', est=30000)
            TH(prog, 'gpsimd', lambda g: g.dma_start(d_nxwbf, d_nextw),
               writes=['cv_nx'], dma_sem='d_cv_nx', est=12000)
            TH(prog, 'vector', lambda v: v.memset(hcarry[:, :], 0.0), writes=['hcarry'])
            TH(prog, 'vector', lambda v: v.memset(halo_sb[:, :], 0.0), writes=['halo'])
            TH(prog, 'scalar', lambda sc: sc.memzero(da0[:, 0::LC + 1]), writes=['da0'])
            TH(prog, 'scalar', lambda sc: sc.memzero(da1[:, 0::LC + 1]), writes=['da1'])
            prog += phase_A(0)
            prog += phase_B(0)

            def stamp(L, t):
                for eng, build, kw in L:
                    kw['slot'] = t
                return L

            for s in range(NCHUNK + 1):
                bg = []
                if s >= 1:
                    bg += phase_D(s - 1) + phase_E(s - 1) + phase_F(s - 1)
                if s + 1 < NCHUNK:
                    bg += phase_A(s + 1) + phase_B(s + 1)
                stamp(bg, s)
                if s < NCHUNK:
                    prog += stamp(phase_C_prologue(s), s)
                    nb = max(1, (len(bg) + 15) // 16)
                    for g_i in range(16):
                        prog += stamp(phase_C_iter(s, g_i), s)
                        prog += bg[g_i * nb:(g_i + 1) * nb]
                    prog += bg[16 * nb:]
                else:
                    prog += bg
            emit_list(schedule_ops(prog))

            final_waits = [(dn, em.cnt[dn]) for dn in DMA_SEMS if em.cnt[dn] > 0]
            em.ops.append(('sync', lambda s: None, final_waits, None, None))

            with nc.Block() as blk1:
                @blk1.sync
                def _(s):
                    em.run('sync', s, sems)

                @blk1.scalar
                def _(s):
                    em.run('scalar', s, sems)

                @blk1.vector
                def _(s):
                    em.run('vector', s, sems)

                @blk1.tensor
                def _(s):
                    em.run('tensor', s, sems)

                @blk1.gpsimd
                def _(s):
                    em.run('gpsimd', s, sems)

    return nc


def _layout_inputs(inputs):
    A = np.ascontiguousarray
    x = inputs['L0_attn_x']
    W_inT = A(inputs['L0_attn_W_in'].T)
    perm = list(range(64, 80)) + list(range(80, 96)) + list(range(0, 64))
    W_xTp = A(inputs['L0_attn_W_x'][perm].T)
    W_dtT = A(inputs['L0_attn_W_dt'].T)
    convw = A(inputs['L0_attn_conv_w'].reshape(16, 128, 4).transpose(1, 0, 2).reshape(128, 64))
    convb = A(inputs['L0_attn_conv_b'].reshape(16, 128).T)
    b_dt = A(inputs['L0_attn_b_dt'].reshape(16, 128).T)
    A_log = A(inputs['L0_attn_A_log'].reshape(16, 128, 16).transpose(1, 0, 2).reshape(128, 256))
    Dp = A(inputs['L0_attn_Dp'].reshape(16, 128).T)
    W_outT = A(inputs['L0_attn_W_out'].T)
    ln_g = A(inputs['L0_ffn_ln_g'].reshape(8, 128).T)
    ln_b = A(inputs['L0_ffn_ln_b'].reshape(8, 128).T)
    attn_w = A(inputs['L0_ffn_attn_w'])
    attn_b = A(inputs['L0_ffn_attn_b'].reshape(8, 128).T)
    w1 = inputs['L0_ffn_w1']
    b1 = inputs['L0_ffn_b1']
    w2 = inputs['L0_ffn_w2']
    b2 = inputs['L0_ffn_b2']
    next_w = A(inputs['L0_ffn_next_w'])
    next_b = A(inputs['L0_ffn_next_b'].reshape(8, 128).T)

    in_maps = []
    for b in range(B_FULL):
        e0 = 2 * (b % 4)
        e1 = e0 + 1
        in_maps.append({
            "x": A(x[b]),
            "W_inT": W_inT,
            "convw": convw,
            "convb": convb,
            "W_xTp": W_xTp,
            "W_dtT": W_dtT,
            "b_dt": b_dt,
            "A_log": A_log,
            "Dp": Dp,
            "W_outT": W_outT,
            "ln_g": ln_g,
            "ln_b": ln_b,
            "attn_w": attn_w,
            "attn_b": attn_b,
            "w1A": A(w1[e0]),
            "w1B": A(w1[e1]),
            "b1A": A(b1[e0].reshape(32, 128).T),
            "b1B": A(b1[e1].reshape(32, 128).T),
            "w2A": A(w2[e0]),
            "w2B": A(w2[e1]),
            "b2A": A(b2[e0].reshape(8, 128).T),
            "b2B": A(b2[e1].reshape(8, 128).T),
            "next_w": next_w,
            "next_b": next_b,
        })
    return in_maps


def kernel(**inputs):
    inputs = {k: np.asarray(v, dtype=np.float32) for k, v in inputs.items()}
    in_maps = _layout_inputs(inputs)
    nc = build_nc()
    res = run_bass_kernel_spmd(nc, in_maps, core_ids=list(range(B_FULL)))
    out = np.stack([res.results[b]["out"] for b in range(B_FULL)], axis=0)
    return out
